# version 02 (final)
# baseline (speedup 1.0000x reference)
import contextlib

import numpy as np
import ml_dtypes
import concourse.bass as bass
import concourse.mybir as mybir
from concourse.bass_utils import run_bass_kernel_spmd

F32 = mybir.dt.float32
BF16 = mybir.dt.bfloat16
F16 = mybir.dt.float16
I32 = mybir.dt.int32
I8 = mybir.dt.int8
AF = mybir.ActivationFunctionType
ALU = mybir.AluOpType
AX = mybir.AxisListType
BF = ml_dtypes.bfloat16

T = 4096
D = 1024
NT = T // 128
ND = D // 128

ENGINES = ("sync", "tensor", "vector", "scalar", "gpsimd")

DBG_FFN_SERIAL = 0
DBG_ATTN_SERIAL = 0


def _prod(xs):
    r = 1
    for x in xs:
        r *= x
    return r


class Prog:
    def __init__(self, nc: bass.Bass):
        self.nc = nc
        self.q = {e: [] for e in ENGINES}
        self.sems = {}
        self.count = {}
        self.stack = contextlib.ExitStack()
        self.tensors = {}

    def sem(self, name):
        if name not in self.sems:
            self.sems[name] = self.stack.enter_context(self.nc.semaphore(name))
            self.count[name] = 0
        return self.sems[name]

    def sbuf(self, name, shape, dtype):
        t = self.stack.enter_context(self.nc.sbuf_tensor(name, list(shape), dtype))
        self.tensors[name] = t
        return t

    def psum(self, name, shape, dtype=F32):
        t = self.stack.enter_context(self.nc.psum_tensor(name, list(shape), dtype))
        self.tensors[name] = t
        return t

    def dram(self, name, shape, dtype, kind="Internal"):
        t = self.nc.dram_tensor(name, list(shape), dtype, kind=kind)
        self.tensors[name] = t
        return t.ap()

    def val(self, semname):
        self.sem(semname)
        return self.count[semname]

    def emit(self, eng, build, waits=(), inc=None):
        waits = [(s, v) for (s, v) in waits if v > 0]
        for s, _ in waits:
            self.sem(s)
        ret = None
        if inc is not None:
            s, amt = inc
            self.sem(s)
            self.count[s] += amt
            ret = self.count[s]

        def f(engobj, _waits=waits, _inc=inc, _build=build):
            for s, v in _waits:
                engobj.wait_ge(self.sems[s], v)
            inst = _build(engobj)
            if _inc is not None:
                if inst is None:
                    inst = engobj.nop()
                inst.then_inc(self.sems[_inc[0]], _inc[1])

        self.q[eng].append(f)
        return ret

    def flow(self, semname):
        return [(semname, self.val(semname))]

    def barrier(self, engines=ENGINES, name="bar"):
        self.sem(name)
        base = self.count[name]
        for e in engines:
            self.count[name] += 1

            def f_inc(engobj):
                engobj.nop().then_inc(self.sems[name], 1)

            self.q[e].append(f_inc)
        target = base + len(engines)
        for e in engines:

            def f_wait(engobj, t=target):
                engobj.wait_ge(self.sems[name], t)

            self.q[e].append(f_wait)
        return target

    def build(self):
        nc = self.nc
        with nc.Block() as block:

            @block.sync
            def _(eng):
                for f in self.q["sync"]:
                    f(eng)

            @block.tensor
            def _(eng):
                for f in self.q["tensor"]:
                    f(eng)

            @block.vector
            def _(eng):
                for f in self.q["vector"]:
                    f(eng)

            @block.scalar
            def _(eng):
                for f in self.q["scalar"]:
                    f(eng)

            @block.gpsimd
            def _(eng):
                for f in self.q["gpsimd"]:
                    f(eng)

        self.stack.close()
        return nc


class Ring:

    def __init__(self, P, name, n, ca=1):
        self.P = P
        self.n = n
        self.ca = ca
        self.ps = [name + f"_p{i}" for i in range(n)]
        self.cs = [name + f"_c{i}" for i in range(n)]
        for s in self.ps + self.cs:
            P.sem(s)
        self.cbase = [P.val(s) for s in self.cs]
        self.fills = 0
        self.drains = 0
        self.ready = []

    def acquire(self):
        s = self.fills % self.n
        t = self.cbase[s] + (self.fills // self.n) * self.ca
        w = [(self.cs[s], t)] if t > 0 else []
        return s, w

    def fill_inc(self, amt):
        return (self.ps[self.fills % self.n], amt)

    def commit(self):
        self.ready.append(self.P.val(self.ps[self.fills % self.n]))
        self.fills += 1

    def cons(self, idx=None):
        i = self.drains if idx is None else idx
        return [(self.ps[i % self.n], self.ready[i])]

    def done(self, amt=None):
        s = self.drains % self.n
        self.drains += 1
        return (self.cs[s], amt if amt is not None else self.ca)

    def cslot(self):
        return self.drains % self.n

    def all_drained_waits(self):
        return [(self.cs[i], self.P.val(self.cs[i])) for i in range(self.n)]


class Arena:

    def __init__(self, P, nbytes=204 * 1024):
        self.t = P.sbuf("ARENA", [128, nbytes], I8)
        self.nbytes = nbytes
        self.off = 0

    def _view(self, off, shape, dtype):
        elems = _prod(shape[1:])
        nb = elems * mybir.dt.size(dtype)
        ap = self.t[0:shape[0], off:off + nb].bitcast(dtype)
        if len(shape) > 2:
            pat = "p (" + " ".join(f"a{i}" for i in range(len(shape) - 1)) + ") -> p " + \
                  " ".join(f"a{i}" for i in range(len(shape) - 1))
            kw = {f"a{i}": shape[1 + i] for i in range(len(shape) - 1)}
            ap = ap.rearrange(pat, **kw)
        return ap

    def alloc(self, shape, dtype, name=None):
        elems = _prod(shape[1:])
        nb = elems * mybir.dt.size(dtype)
        off = (self.off + 63) & ~63
        self.off = off + nb
        assert self.off <= self.nbytes, f"SBUF arena overflow: {self.off}"
        return self._view(off, shape, dtype)

    def at(self, off, shape, dtype):
        return self._view(off, shape, dtype)

    def mark(self):
        return self.off

    def reset(self, m):
        self.off = m


class Res:

    def __init__(self, P):
        self.P = P
        self.mmS = [P.psum(f"mmS{i}", [128, 512], F32) for i in range(4)]
        self.psT = P.psum("psT", [128, 1024], BF16)
        self.psT2 = P.psum("psT2", [128, 1024], BF16)
        self.psC = P.psum("psC", [128, 512], F32)
        self.psM = P.psum("psM", [128, 512], F32)
        self.mm_slots = [t[:, :] for t in self.mmS]
        self.arena = Arena(P)
        self.eye_bf = P.sbuf("eye_bf", [128, 128], BF16)
        self.eye_f = P.sbuf("eye_f", [128, 128], F32)
        self.ones_bf = P.sbuf("ones_bf", [1, 512], BF16)
        self.nones_bf = P.sbuf("nones_bf", [1, 128], BF16)

    def init_consts(self, P, eye_dram):
        P.emit("sync", lambda e: e.dma_start(self.eye_f[:, :], eye_dram[:, :]),
               inc=("dS", 16))
        P.emit("gpsimd", lambda e: e.dma_start(self.eye_bf[:, :], eye_dram[:, :]),
               inc=("dG", 16))
        P.emit("vector", lambda e: e.memset(self.ones_bf[:, :], 1.0), inc=("lv", 1))
        P.emit("vector", lambda e: e.memset(self.nones_bf[:, :], -1.0), inc=("lv", 1))
        P.emit("sync", lambda e: None, waits=P.flow("dS"))
        P.emit("gpsimd", lambda e: None, waits=P.flow("dG"))


def emit_prepass_transpose(P, R, src_ap, FF, EE, name, src_dtype=F32,
                           row_scale=None):
    P.barrier(name="bar")
    wbf = P.dram(name + "_bf", [FF, EE], BF16)
    wT = P.dram(name + "_T", [EE, FF], BF16)
    nf, ne = FF // 128, EE // 128
    m = R.arena.mark()
    stg = R.arena.alloc([128, 2, EE], BF16)
    rB = Ring(P, "ppB", 2, ca=16)
    if src_dtype != F32:
        stgi = R.arena.alloc([128, 2, EE], src_dtype)
        rA = Ring(P, "ppA", 2, ca=1)
    for c in range(nf):
        if src_dtype == F32:
            s, w = rB.acquire()
            P.emit("gpsimd", lambda e, c=c, s=s: e.dma_start(
                stg[:, s, :], src_ap[128 * c:128 * (c + 1), :]),
                waits=w, inc=rB.fill_inc(16))
            extra_w = []
            if row_scale is not None:
                vd = P.val(rB.ps[rB.fills % rB.n])
                vv = P.emit("vector", lambda e, c=c, s=s: e.tensor_scalar_mul(
                    stg[:, s, :], stg[:, s, :], row_scale(c)),
                    waits=[(rB.ps[rB.fills % rB.n], vd)], inc=("ppsc", 1))
                extra_w = [("ppsc", vv)]
            rB.commit()
            P.emit("sync", lambda e, c=c, s=rB.cslot(): e.dma_start(
                wbf[128 * c:128 * (c + 1), :], stg[:, s, :]),
                waits=rB.cons() + extra_w, inc=rB.done())
        else:
            sa, wa = rA.acquire()
            P.emit("sync", lambda e, c=c, s=sa: e.dma_start(
                stgi[:, s, :], src_ap[128 * c:128 * (c + 1), :]),
                waits=wa, inc=rA.fill_inc(16))
            rA.commit()
            sb, wb = rB.acquire()
            P.emit("vector", lambda e, sa=rA.cslot(), sb=sb: e.tensor_copy(
                stg[:, sb, :], stgi[:, sa, :]),
                waits=rA.cons() + wb, inc=rB.fill_inc(1))
            P.emit("vector", lambda e: None, inc=rA.done())
            if row_scale is not None:
                P.emit("vector", lambda e, c=c, s=sb: e.tensor_scalar_mul(
                    stg[:, s, :], stg[:, s, :], row_scale(c)), inc=rB.fill_inc(1))
            rB.commit()
            P.emit("sync", lambda e, c=c, s=rB.cslot(): e.dma_start(
                wbf[128 * c:128 * (c + 1), :], stg[:, s, :]),
                waits=rB.cons(), inc=rB.done())
    st_done_w = rB.all_drained_waits()
    tstg = R.arena.alloc([128, 2, FF], BF16)
    rT = Ring(P, "ppT", 2, ca=16)
    for c in range(ne):
        s, w = rT.acquire()
        if c == 0:
            w = w + st_done_w
        P.emit("sync", lambda e, c=c, s=s: e.dma_start_transpose(
            tstg[:, s, :], wbf[:, 128 * c:128 * (c + 1)]),
            waits=w, inc=rT.fill_inc(16))
        rT.commit()
        P.emit("sync", lambda e, c=c, s=rT.cslot(): e.dma_start(
            wT[128 * c:128 * (c + 1), :], tstg[:, s, :]),
            waits=rT.cons(), inc=rT.done())
    P.emit("sync", lambda e: None, waits=rT.all_drained_waits())
    R.arena.reset(m)
    return wT, rT.all_drained_waits()


def emit_ln_tiles(P, R, src_ap, t0, t1, consume, shift=False, ln=True,
                  dma_eng="sync", xcb=None, out_f32=False):
    a = R.arena
    m = a.mark()
    xf = a.alloc([128, 2, D], F32)
    lt = a.alloc([128, 2, D], F32 if out_f32 else BF16)
    st = a.alloc([128, 8], F32)
    sq = a.alloc([128, D], F32)
    rx = Ring(P, "lnx", 2, ca=1)
    rl = Ring(P, "lnl", 2, ca=1)
    dsem = "dS" if dma_eng == "sync" else "dG"
    for i, tt in enumerate(range(t0 // 128, t1 // 128)):
        s, w = rx.acquire()
        if not shift:
            P.emit(dma_eng, lambda e, tt=tt, s=s: e.dma_start(
                xf[:, s, :], src_ap[128 * tt:128 * (tt + 1), :]),
                waits=w, inc=rx.fill_inc(16))
        else:
            P.emit(dma_eng, lambda e, tt=tt, s=s: e.dma_start(
                xf[:, s, 0:D // 2], src_ap[128 * tt:128 * (tt + 1), 0:D // 2]),
                waits=w, inc=rx.fill_inc(16))
            if tt == 0:
                P.emit(dma_eng, lambda e, s=s: e.dma_start(
                    xf[1:128, s, D // 2:D], src_ap[0:127, D // 2:D]),
                    inc=rx.fill_inc(16))
                P.emit("vector", lambda e, s=s: e.memset(xf[0:1, s, D // 2:D], 0.0),
                       waits=w, inc=rx.fill_inc(1))
            else:
                P.emit(dma_eng, lambda e, tt=tt, s=s: e.dma_start(
                    xf[:, s, D // 2:D],
                    src_ap[128 * tt - 1:128 * (tt + 1) - 1, D // 2:D]),
                    inc=rx.fill_inc(16))
        rx.commit()
        s2, w2 = rl.acquire()
        wx = rx.cons()
        if not ln:
            P.emit("vector", lambda e, s=s, s2=s2: e.tensor_copy(lt[:, s2, :], xf[:, s, :]),
                   waits=wx + w2, inc=rl.fill_inc(1))
            P.emit("vector", lambda e: None, inc=rx.done())
        else:
            P.emit("vector", lambda e, s=s: e.reduce_sum(st[:, 0:1], xf[:, s, :], axis=AX.X),
                   waits=wx + w2 + P.flow("ls"), inc=("lv", 1))
            P.emit("vector", lambda e: e.tensor_scalar_mul(st[:, 1:2], st[:, 0:1], -1.0 / D),
                   waits=P.flow("lv"), inc=("lv", 1))
            P.emit("vector", lambda e, s=s: e.tensor_scalar_add(xf[:, s, :], xf[:, s, :],
                                                                st[:, 1:2]),
                   waits=P.flow("lv"), inc=("lv", 1))
            P.emit("scalar", lambda e, s=s: e.activation(sq[:, :], xf[:, s, :], AF.Square,
                                                         accum_out=st[:, 2:3]),
                   waits=P.flow("lv"), inc=("ls", 1))
            P.emit("vector", lambda e: e.tensor_scalar_mul(st[:, 3:4], st[:, 2:3], 1.0 / D),
                   waits=P.flow("ls"), inc=("lv", 1))
            P.emit("vector", lambda e: e.tensor_scalar_add(st[:, 3:4], st[:, 3:4], 1e-5),
                   waits=P.flow("lv"), inc=("lv", 1))
            P.emit("scalar", lambda e: e.activation(st[:, 4:5], st[:, 3:4], AF.Sqrt),
                   waits=P.flow("lv"), inc=("ls", 1))
            P.emit("vector", lambda e: e.reciprocal(st[:, 5:6], st[:, 4:5]),
                   waits=P.flow("ls"), inc=("lv", 1))
            fi = rl.fill_inc(1)
            v = P.emit("vector", lambda e, s=s, s2=s2: e.tensor_scalar_mul(
                lt[:, s2, :], xf[:, s, :], st[:, 5:6]), waits=P.flow("lv"), inc=fi)
            if xcb is not None:
                xcb(i, xf, s, st)
            P.emit("vector", lambda e: None, waits=[(fi[0], v)], inc=rx.done())
        rl.commit()
        consume(i, lt, s2, rl)
    a.reset(m)


def emit_transposes(P, R, pairs, extra_waits=(), dst_dtype=None):
    rt = Ring(P, "trp", 2, ca=1)
    for g in range(0, len(pairs), 4):
        gn = min(4, len(pairs) - g)
        s, w = rt.acquire()
        bank = R.psT if s == 0 else R.psT2
        if g == 0:
            w = list(extra_waits) + w
        for k in range(gn):
            src, _ = pairs[g + k]
            P.emit("tensor", lambda e, src=src, k=k, bank=bank: e.transpose(
                bank[:, 128 * k:128 * (k + 1)], src, R.eye_bf[:, :]),
                waits=(w if k == 0 else []),
                inc=(rt.fill_inc(1) if k == gn - 1 else None))
        rt.commit()
        wv = rt.cons()
        for k in range(gn):
            _, dst = pairs[g + k]
            P.emit("vector", lambda e, dst=dst, k=k, bank=bank: e.tensor_copy(
                dst, bank[:, 128 * k:128 * (k + 1)]),
                waits=(wv if k == 0 else []),
                inc=(rt.done() if k == gn - 1 else None))


def build_L1_ffn(P, R, x_in, x_out, wg_d, wu_d, wd_d):
    SC = 1.0 / 127.0
    wgT, wg_done = emit_prepass_transpose(P, R, wg_d, 3072, D, "l1wg", I32)
    wuT, wu_done = emit_prepass_transpose(P, R, wu_d, 3072, D, "l1wu", I32)
    wdT, wd_done = emit_prepass_transpose(P, R, wd_d, D, 3072, "l1wd", I32)
    P.barrier(name="bar")

    TG = 1024
    a = R.arena
    m = a.mark()
    lnT = a.alloc([128, ND, TG], BF16)
    hbuf = a.alloc([128, 24, TG], BF16)
    wbuf = a.alloc([128, 24576], BF16)
    xres = a.alloc([128, 2, D], F32)

    for sg in range(T // TG):
        t0 = sg * TG

        def consume(i, lt, s2, rl):
            pairs = [(lt[:, s2, 128 * c:128 * (c + 1)],
                      lnT[:, c, 128 * i:128 * (i + 1)]) for c in range(ND)]
            emit_transposes(P, R, pairs, extra_waits=rl.cons())
            P.emit("vector", lambda e: None, inc=rl.done())

        emit_ln_tiles(P, R, x_in, t0, t0 + TG, consume)
        P.barrier(name="bar")

        wv8 = wbuf.rearrange("p (a b) -> p a b", a=8)
        rp = Ring(P, "mm", 4, ca=1)

        def proj_phase(wT_ap, wdone, first, epilog):
            for c in range(8):
                w = wdone if first and c == 0 else []
                P.emit("sync", lambda e, c=c: e.dma_start(
                    wv8[:, c, :], wT_ap[128 * c:128 * (c + 1), :]),
                    waits=w, inc=("lw", 16))
            wgv = P.val("lw")
            for fc in range(24):
                for tg in range(TG // 512):
                    sl, w = rp.acquire()
                    if fc == 0 and tg == 0:
                        w = w + [("lw", wgv)]
                    for ec in range(8):
                        P.emit("tensor", lambda e, fc=fc, tg=tg, ec=ec, sl=sl: e.matmul(
                            R.mm_slots[sl][:, :], wv8[:, ec, 128 * fc:128 * (fc + 1)],
                            lnT[:, ec, 512 * tg:512 * (tg + 1)],
                            start=(ec == 0), stop=(ec == 7)),
                            waits=(w if ec == 0 else []),
                            inc=(rp.fill_inc(1) if ec == 7 else None))
                    rp.commit()
                    epilog(fc, tg, rp)

        def gate_epilog(fc, tg, rp):
            P.emit("scalar", lambda e, fc=fc, tg=tg, sl=rp.cslot(): e.activation(
                hbuf[:, fc, 512 * tg:512 * (tg + 1)], R.mm_slots[sl][:, :],
                AF.Silu, scale=SC),
                waits=rp.cons(), inc=rp.done())

        proj_phase(wgT, wg_done, sg == 0, gate_epilog)
        P.barrier(name="bar")

        def up_epilog(fc, tg, rp):
            P.emit("vector", lambda e, fc=fc, tg=tg, sl=rp.cslot(): e.scalar_tensor_tensor(
                hbuf[:, fc, 512 * tg:512 * (tg + 1)], R.mm_slots[sl][:, :], SC,
                hbuf[:, fc, 512 * tg:512 * (tg + 1)], op0=ALU.mult, op1=ALU.mult),
                waits=rp.cons(), inc=rp.done())

        proj_phase(wuT, wu_done, sg == 0, up_epilog)
        P.barrier(name="bar")

        wv24 = wbuf.rearrange("p (a b) -> p a b", a=24)
        for c in range(24):
            w = wd_done if sg == 0 and c == 0 else []
            P.emit("sync", lambda e, c=c: e.dma_start(
                wv24[:, c, :], wdT[128 * c:128 * (c + 1), :]),
                waits=w, inc=("lw", 16))
        wgv = P.val("lw")
        rxr = Ring(P, "rxr", 2, ca=16)
        for tt in range(TG // 128):
            sx, wx = rxr.acquire()
            P.emit("sync", lambda e, tt=tt, sx=sx, t0=t0: e.dma_start(
                xres[:, sx, :], x_in[t0 + 128 * tt:t0 + 128 * (tt + 1), :]),
                waits=wx, inc=rxr.fill_inc(16))
            rxr.commit()
            stt_w = []
            for dg in range(2):
                sl, w = rp.acquire()
                if tt == 0 and dg == 0:
                    w = w + [("lw", wgv)]
                for hc in range(24):
                    P.emit("tensor", lambda e, tt=tt, dg=dg, hc=hc, sl=sl: e.matmul(
                        R.mm_slots[sl][:, :],
                        hbuf[:, hc, 128 * tt:128 * (tt + 1)],
                        wv24[:, hc, 512 * dg:512 * (dg + 1)],
                        start=(hc == 0), stop=(hc == 23)),
                        waits=(w if hc == 0 else []),
                        inc=(rp.fill_inc(1) if hc == 23 else None))
                rp.commit()
                wv = rp.cons() + (rxr.cons() if dg == 0 else [])
                csl = rp.cslot()
                d = rp.done()
                v = P.emit("vector", lambda e, dg=dg, sl=csl, sx=rxr.cslot():
                           e.scalar_tensor_tensor(
                               xres[:, sx, 512 * dg:512 * (dg + 1)], R.mm_slots[sl][:, :],
                               SC, xres[:, sx, 512 * dg:512 * (dg + 1)],
                               op0=ALU.mult, op1=ALU.add),
                           waits=wv, inc=d)
                stt_w.append((d[0], v))
            P.emit("sync", lambda e, tt=tt, sx=rxr.cslot(), t0=t0: e.dma_start(
                x_out[t0 + 128 * tt:t0 + 128 * (tt + 1), :], xres[:, sx, :]),
                waits=stt_w, inc=rxr.done())
        P.barrier(name="bar")
    P.emit("gpsimd", lambda e: None, waits=[("rxr_c0", P.val("rxr_c0")), ("rxr_c1", P.val("rxr_c1"))])
    a.reset(m)


def emit_colstage(P, R, vec_ap, N, dest):
    nch = N // 128
    a = R.arena
    m = a.mark()
    stg = a.alloc([nch, 128], F32)
    P.emit("sync", lambda e: e.dma_start(
        stg[:, :], vec_ap.rearrange("(a b) -> a b", b=128)),
        waits=P.flow("lt"), inc=("dS", 16))
    P.emit("sync", lambda e: None, waits=P.flow("dS"))
    P.emit("tensor", lambda e: e.transpose(R.psM[:, 0:nch], stg[:, :],
                                           R.eye_f[0:nch, 0:nch]),
           waits=P.flow("dS") + P.flow("lv"), inc=("lt", 1))
    P.emit("vector", lambda e: e.tensor_copy(dest, R.psM[:, 0:nch]),
           waits=P.flow("lt"), inc=("lv", 1))
    a.reset(m)


def build_ternary_ffn(P, R, x_in, x_out, wd, tag):
    DH = 4096
    a = R.arena
    m0 = a.mark()
    scol = a.alloc([128, 136], F32)
    for i, nm in enumerate(("s1", "s2", "s1h", "s2h")):
        emit_colstage(P, R, wd[nm], DH, scol[:, 32 * i:32 * (i + 1)])
    emit_colstage(P, R, wd["s3"], D, scol[:, 128:136])
    P.barrier(name="bar")

    w1T, w1dw = emit_prepass_transpose(P, R, wd["w1"], DH, D, tag + "w1")
    w2T, w2dw = emit_prepass_transpose(P, R, wd["w2"], DH, D, tag + "w2")
    h1T, h1dw = emit_prepass_transpose(P, R, wd["h1"], DH, DH, tag + "h1")
    h2T, h2dw = emit_prepass_transpose(P, R, wd["h2"], DH, DH, tag + "h2")
    w3T, w3dw = emit_prepass_transpose(P, R, wd["w3"], D, DH, tag + "w3",
                                       row_scale=lambda c: scol[:, 128 + c:129 + c])
    P.barrier(name="bar")

    TG = 1024
    hA = a.alloc([128, 32, TG], BF16)
    hB = a.alloc([128, 32, TG], BF16)
    u1 = a.mark()
    wcolH = a.alloc([128, 2, 32, 128], BF16)
    wcolA = a.at((u1 + 63) & ~63, [128, 2, 2, 8, 128], BF16)
    u2 = a.mark()
    w3buf = a.alloc([128, 32, 512], BF16)
    lnT = a.at((u2 + 63) & ~63, [128, ND, TG], BF16)
    tmp = a.alloc([128, 512], F32)
    xres = a.alloc([128, 2, 512], F32)
    rw = Ring(P, "wc", 2, ca=1)
    rp = Ring(P, "mm", 4, ca=1)
    rxr = Ring(P, "rxr", 2, ca=16)
    prev_stt = []

    for sg in range(T // TG):
        t0 = sg * TG

        def consume(i, lt, s2, rl):
            pairs = [(lt[:, s2, 128 * c:128 * (c + 1)],
                      lnT[:, c, 128 * i:128 * (i + 1)]) for c in range(ND)]
            emit_transposes(P, R, pairs, extra_waits=rl.cons())
            P.emit("vector", lambda e: None, inc=rl.done())

        emit_ln_tiles(P, R, x_in, t0, t0 + TG, consume)
        P.barrier(name="bar")

        for fc in range(32):
            sw, ww = rw.acquire()
            for j, (wTn, dw) in enumerate(((tag + "w1_T", w1dw), (tag + "w2_T", w2dw))):
                src = bass.AP(P.tensors[wTn], 128 * fc,
                              [[DH, 128], [DH * 128, 8], [1, 128]])
                wj = list(ww) if j == 0 else []
                if sg == 0 and fc == 0:
                    wj = wj + dw
                P.emit("sync", lambda e, src=src, sw=sw, j=j: e.dma_start(
                    wcolA[:, sw, j, :, :], src), waits=wj, inc=rw.fill_inc(16))
            rw.commit()
            rel = []
            for tg in range(2):
                sls = []
                for j in range(2):
                    sl, w = rp.acquire()
                    if tg == 0 and j == 0:
                        w = w + rw.cons()
                    for ec in range(8):
                        P.emit("tensor", lambda e, fc=fc, tg=tg, ec=ec, sl=sl, j=j,
                               sw=rw.cslot(): e.matmul(
                            R.mm_slots[sl][:, :], wcolA[:, sw, j, ec, :],
                            lnT[:, ec, 512 * tg:512 * (tg + 1)],
                            start=(ec == 0), stop=(ec == 7)),
                            waits=(w if ec == 0 else []),
                            inc=(rp.fill_inc(1) if ec == 7 else None))
                    rp.commit()
                    sls.append(sl)
                wv = rp.cons() + prev_stt
                P.emit("scalar", lambda e, fc=fc, sl=sls[0]: e.activation(
                    tmp[:, :], R.mm_slots[sl][:, :], AF.Silu,
                    scale=scol[:, fc:fc + 1]),
                    waits=wv, inc=("ls", 1))
                dA = rp.done()
                P.emit("scalar", lambda e: None, waits=P.flow("ls"), inc=dA)
                wv = rp.cons() + P.flow("ls")
                dB = rp.done()
                v = P.emit("vector", lambda e, fc=fc, tg=tg, sl=sls[1]:
                           e.scalar_tensor_tensor(
                               hA[:, fc, 512 * tg:512 * (tg + 1)],
                               R.mm_slots[sl][:, :], scol[:, 32 + fc:33 + fc],
                               tmp[:, :], op0=ALU.mult, op1=ALU.mult),
                           waits=wv, inc=dB)
                prev_stt = [(dB[0], v)]
                rel.append((dB[0], v))
            P.emit("vector", lambda e: None, waits=rel, inc=rw.done())
        P.barrier(name="bar")

        for stage, (wTn, hdw, src_h, dst_h, so) in enumerate((
                (tag + "h1_T", h1dw, hA, hB, 64),
                (tag + "h2_T", h2dw, hB, hA, 96))):
            for fc in range(32):
                sw, ww = rw.acquire()
                src = bass.AP(P.tensors[wTn], 128 * fc,
                              [[DH, 128], [DH * 128, 32], [1, 128]])
                wj = list(ww)
                if sg == 0 and fc == 0:
                    wj = wj + hdw
                P.emit("sync", lambda e, src=src, sw=sw: e.dma_start(
                    wcolH[:, sw, :, :], src), waits=wj, inc=rw.fill_inc(16))
                rw.commit()
                rel = []
                for tg in range(2):
                    sl, w = rp.acquire()
                    if tg == 0:
                        w = w + rw.cons()
                    for k in range(32):
                        P.emit("tensor", lambda e, fc=fc, tg=tg, k=k, sl=sl,
                               sw=rw.cslot(), src_h=src_h: e.matmul(
                            R.mm_slots[sl][:, :], wcolH[:, sw, k, :],
                            src_h[:, k, 512 * tg:512 * (tg + 1)],
                            start=(k == 0), stop=(k == 31)),
                            waits=(w if k == 0 else []),
                            inc=(rp.fill_inc(1) if k == 31 else None))
                    rp.commit()
                    d = rp.done()
                    v = P.emit("scalar", lambda e, fc=fc, tg=tg, sl=rp.cslot() if False else sl,
                               dst_h=dst_h, so=so: e.activation(
                        dst_h[:, fc, 512 * tg:512 * (tg + 1)], R.mm_slots[sl][:, :],
                        AF.Gelu, scale=scol[:, so + fc:so + fc + 1]),
                        waits=rp.cons(idx=rp.drains - 1), inc=d)
                    rel.append((d[0], v))
                P.emit("vector", lambda e: None, waits=rel, inc=rw.done())
            P.barrier(name="bar")

        for dg in range(2):
            src = bass.AP(P.tensors[tag + "w3_T"], 512 * dg,
                          [[D, 128], [D * 128, 32], [1, 512]])
            wj = list(w3dw) if sg == 0 and dg == 0 else []
            P.emit("sync", lambda e, src=src: e.dma_start(w3buf[:, :, :], src),
                   waits=wj, inc=("lw", 16))
            lwv = P.val("lw")
            for tt in range(TG // 128):
                sx, wx = rxr.acquire()
                P.emit("sync", lambda e, tt=tt, sx=sx, t0=t0, dg=dg: e.dma_start(
                    xres[:, sx, :],
                    x_in[t0 + 128 * tt:t0 + 128 * (tt + 1), 512 * dg:512 * (dg + 1)]),
                    waits=wx, inc=rxr.fill_inc(16))
                rxr.commit()
                sl, w = rp.acquire()
                if tt == 0:
                    w = w + [("lw", lwv)]
                for k in range(32):
                    P.emit("tensor", lambda e, tt=tt, k=k, sl=sl: e.matmul(
                        R.mm_slots[sl][:, :], hA[:, k, 128 * tt:128 * (tt + 1)],
                        w3buf[:, k, :], start=(k == 0), stop=(k == 31)),
                        waits=(w if k == 0 else []),
                        inc=(rp.fill_inc(1) if k == 31 else None))
                rp.commit()
                wv = rp.cons() + rxr.cons()
                d = rp.done()
                v = P.emit("vector", lambda e, sl=rp.cslot() if False else sl,
                           sx=rxr.cslot(): e.scalar_tensor_tensor(
                    xres[:, sx, :], R.mm_slots[sl][:, :], 1.0,
                    xres[:, sx, :], op0=ALU.mult, op1=ALU.add),
                    waits=wv, inc=d)
                P.emit("sync", lambda e, tt=tt, sx=rxr.cslot(), t0=t0, dg=dg: e.dma_start(
                    x_out[t0 + 128 * tt:t0 + 128 * (tt + 1), 512 * dg:512 * (dg + 1)],
                    xres[:, sx, :]),
                    waits=[(d[0], v)], inc=rxr.done())
            P.barrier(name="bar")
    P.emit("gpsimd", lambda e: None, waits=rxr.all_drained_waits())
    a.reset(m0)


def emit_attn_core(P, R, qsp, ksp, Vall, ctx_all, nheads=16, dh_qk=64, dh_v=64,
                   causal=True, IW=512, qk_load=None, ctx_slice=None,
                   vh_map=None, qk_bufs=None):
    a = R.arena
    m = a.mark()
    NC = T // 128
    if qk_bufs is None:
        qh = a.alloc([dh_qk, 1, T], BF16)
        kh = a.alloc([dh_qk, 1, T], BF16)
    else:
        qh = qk_bufs[0].rearrange("p (a n) -> p a n", a=1)
        kh = qk_bufs[1].rearrange("p (a n) -> p a n", a=1)
    Eb = a.alloc([128, NC, IW], BF16)
    Nmat = a.alloc([128, 32], F32)
    Rbuf = a.alloc([128, 2, 34], F32)
    Vs = a.alloc([128, 1, NC, dh_v + 1], BF16)
    bc_sb = a.alloc([dh_v, IW], F32)
    linv = a.alloc([1, IW], BF16)
    sc1 = a.alloc([128, 4], F32)
    sc2 = a.alloc([1, 4], BF16)
    rqk = Ring(P, "rqk", 1, ca=1)
    rE = Ring(P, "rE", 1, ca=1)
    rvs = Ring(P, "rvs", 1, ca=1)
    rcx = Ring(P, "rcx", 1, ca=1)
    rp = Ring(P, "mm", 4, ca=1)
    n_ig = T // IW
    if ctx_slice is None:
        def ctx_slice(h, i0):
            return ctx_all[64 * (h % 2):64 * (h % 2) + 64, h // 2, i0:i0 + IW]
    if vh_map is None:
        vh_map = lambda h: h

    for h in range(nheads):
        sq, wq = rqk.acquire()
        if qk_load is None:
            P.emit("sync", lambda e, h=h, sq=sq: e.dma_start(
                qh[:, sq, :], qsp[dh_qk * h:dh_qk * (h + 1), :]),
                waits=wq, inc=rqk.fill_inc(16))
            P.emit("sync", lambda e, h=h, sq=sq: e.dma_start(
                kh[:, sq, :], ksp[dh_qk * h:dh_qk * (h + 1), :]),
                inc=rqk.fill_inc(16))
        else:
            qk_load(h, sq, wq, qh, kh, rqk, rp)
        rqk.commit()
        wqk = rqk.cons()
        first_of_head = True
        for ig in range(n_ig):
            i0 = ig * IW
            ncv = min(NC, (i0 + IW + 127) // 128) if causal else NC
            sE, wE = rE.acquire()
            for c in range(ncv):
                sl, w = rp.acquire()
                if first_of_head:
                    w = w + wqk
                    first_of_head = False
                diag = causal and (128 * (c + 1) > i0)
                P.emit("tensor", lambda e, c=c, sl=sl, sq=sq, i0=i0: e.matmul(
                    R.mm_slots[sl][:, 0:IW], kh[:, sq, 128 * c:128 * (c + 1)],
                    qh[:, sq, i0:i0 + IW], start=True, stop=True),
                    waits=w + (wE if c == 0 else []), inc=rp.fill_inc(1))
                rp.commit()
                P.emit("vector", lambda e, c=c, sl=sl: e.reduce_max(
                    Nmat[:, c:c + 1], R.mm_slots[sl][:, 0:IW], axis=AX.X, negate=True),
                    waits=rp.cons(idx=rp.drains) + P.flow("ls"), inc=("lv", 1))
                d = rp.done()
                P.emit("scalar", lambda e, c=c, sl=sl: e.activation(
                    Eb[:, c, :], R.mm_slots[sl][:, 0:IW], AF.Exp,
                    bias=Nmat[:, c:c + 1]),
                    waits=rp.cons(idx=rp.drains - 1) + P.flow("lv"), inc=d)
                if diag:
                    P.emit("gpsimd", lambda e, c=c, i0=i0: e.affine_select(
                        Eb[:, c, :], Eb[:, c, :], [[1, IW]], ALU.is_ge, 0.0,
                        base=i0 - 128 * c, channel_multiplier=-1),
                        waits=[(d[0], P.val(d[0]))], inc=("lg", 1))
            sv, wv = rvs.acquire()
            P.emit("vector", lambda e, ncv=ncv: e.tensor_reduce(
                sc1[:, 0:1], Nmat[:, 0:ncv], axis=AX.X, op=ALU.min),
                waits=P.flow("lv") + P.flow("ls") + P.flow("lt"), inc=("lv", 1))
            P.emit("tensor", lambda e: e.transpose(
                R.psM[0:1, 0:128], sc1[:, 0:1], R.eye_f[:, :]),
                waits=P.flow("lv") + P.flow("ls"), inc=("lt", 1))
            P.emit("vector", lambda e: e.tensor_reduce(
                sc2[0:1, 0:1], R.psM[0:1, 0:128], axis=AX.X, op=ALU.min),
                waits=P.flow("lt"), inc=("lv", 1))
            P.emit("tensor", lambda e: e.matmul(
                R.psM[:, 256:257], R.ones_bf[0:1, 0:128], sc2[0:1, 0:1],
                start=True, stop=True),
                waits=P.flow("lv"), inc=("lt", 1))
            P.emit("scalar", lambda e, sv=sv: e.copy(Rbuf[:, sv, 32:33],
                                                     R.psM[:, 256:257]),
                   waits=P.flow("lt") + wv, inc=("ls", 1))
            P.emit("scalar", lambda e, ncv=ncv, sv=sv: e.activation(
                Rbuf[:, sv, 0:ncv], Nmat[:, 0:ncv], AF.Exp, scale=-1.0,
                bias=Rbuf[:, sv, 32:33]),
                waits=P.flow("ls"), inc=("ls", 1))
            P.emit("vector", lambda e, h=vh_map(h), ncv=ncv, sv=sv: e.tensor_tensor(
                Vs[:, sv, 0:ncv, :], Vall[:, 0:ncv, h, :],
                Rbuf[:, sv, 0:ncv].unsqueeze(2).broadcast_to([128, ncv, dh_v + 1]),
                op=ALU.mult),
                waits=P.flow("ls") + P.flow("Vrdy"), inc=rvs.fill_inc(1))
            rvs.commit()
            scx, wcx = rcx.acquire()
            wfirst = rvs.cons() + P.flow("lg") + wcx
            for c in range(ncv):
                P.emit("tensor", lambda e, c=c, sv=0, ncv=ncv: e.matmul(
                    R.psC[0:dh_v + 1, 0:IW], Vs[:, sv, c, :], Eb[:, c, :],
                    start=(c == 0), stop=(c == ncv - 1)),
                    waits=(wfirst if c == 0 else []),
                    inc=(("lt", 1) if c == ncv - 1 else None))
            rcx.commit()
            P.emit("vector", lambda e: None, waits=P.flow("lt"), inc=rE.done())
            P.emit("vector", lambda e: None, inc=rvs.done())
            def _recip(e):
                with P.nc.allow_low_precision(reason="1/l bcast operand is bf16"):
                    return e.reciprocal(linv[0:1, :], R.psC[dh_v:dh_v + 1, 0:IW])
            P.emit("vector", _recip, waits=P.flow("lt"), inc=("lv", 1))
            P.emit("tensor", lambda e: e.matmul(
                R.psM[0:dh_v, 0:IW], R.ones_bf[0:1, 0:dh_v], linv[0:1, :],
                start=True, stop=True),
                waits=P.flow("lv") + P.flow("ls"), inc=("lt", 1))
            P.emit("scalar", lambda e: e.copy(bc_sb[:, :], R.psM[0:dh_v, 0:IW]),
                   waits=P.flow("lt"), inc=("ls", 1))
            d = rcx.done()
            v = P.emit("vector", lambda e, h=h, i0=i0: e.scalar_tensor_tensor(
                ctx_slice(h, i0), R.psC[0:dh_v, 0:IW],
                1.0, bc_sb[:, :], op0=ALU.mult, op1=ALU.mult),
                waits=P.flow("ls"), inc=d)
        P.emit("vector", lambda e: None, waits=[(d[0], v)], inc=rqk.done())
    a.reset(m)


def build_L2_attn(P, R, x_in, x_out, qw_d, kw_d, vw_d, ow_d,
                  rA_d, rB_d, rAk_d, rBk_d, rsgnT_d):
    a = R.arena
    m0 = a.mark()
    lnT = a.alloc([128, ND, T], BF16)
    Vall = a.alloc([128, 32, 16, 65], BF16)
    mQ = a.mark()
    wbuf = a.alloc([128, 8, 1024], BF16)
    qchunk = a.alloc([128, T], BF16)
    qwork = a.alloc([128, 2, 512], BF16)
    tmpf = a.alloc([128, 512], F32)
    tmpg = a.alloc([128, 512], F32)
    tA = a.alloc([128, T], BF16)
    tB = a.alloc([128, T], BF16)
    xres = a.alloc([128, 2, D], F32)
    qsp = P.dram("l2a_qsp", [D, T], BF16)
    ksp = P.dram("l2a_ksp", [D, T], BF16)
    rsgn = P.sbuf("l2a_rsgn", [128, 128], BF16)

    def consume(i, lt, s2, rl):
        pairs = [(lt[:, s2, 128 * c:128 * (c + 1)],
                  lnT[:, c, 128 * i:128 * (i + 1)]) for c in range(ND)]
        emit_transposes(P, R, pairs, extra_waits=rl.cons())
        P.emit("vector", lambda e: None, inc=rl.done())

    emit_ln_tiles(P, R, x_in, 0, T, consume)
    P.emit("gpsimd", lambda e: e.dma_start(rsgn[:, :], rsgnT_d[:, :]), inc=("dG", 16))
    P.barrier(name="bar")

    P.emit("vector", lambda e: e.memset(Vall[:, :, :, 64:65], 1.0), inc=("lv", 1))
    for c in range(8):
        P.emit("gpsimd", lambda e, c=c: e.dma_start(
            wbuf[:, c, :], vw_d[128 * c:128 * (c + 1), :]), inc=("lG", 16))
    lwv = P.val("lG")
    rp = Ring(P, "mm", 4, ca=1)
    for tt in range(NT):
        for hf in range(2):
            sl, w = rp.acquire()
            if tt == 0 and hf == 0:
                w = w + [("lG", lwv)]
            for ec in range(8):
                P.emit("tensor", lambda e, tt=tt, hf=hf, ec=ec, sl=sl: e.matmul(
                    R.mm_slots[sl][:, :], lnT[:, ec, 128 * tt:128 * (tt + 1)],
                    wbuf[:, ec, 512 * hf:512 * (hf + 1)],
                    start=(ec == 0), stop=(ec == 7)),
                    waits=(w if ec == 0 else []),
                    inc=(rp.fill_inc(1) if ec == 7 else None))
            rp.commit()
            d = rp.done()
            P.emit("vector", lambda e, tt=tt, hf=hf, sl=sl: e.tensor_copy(
                Vall[:, tt, 8 * hf:8 * (hf + 1), 0:64], R.mm_slots[sl][:, :]),
                waits=rp.cons(idx=rp.drains - 1), inc=d)
    P.emit("vector", lambda e: None, waits=P.flow("lv"), inc=("Vrdy", 1))
    P.barrier(name="bar")

    for which, (w_d, sp, tA_d, tB_d) in enumerate(
            ((qw_d, qsp, rA_d, rB_d), (kw_d, ksp, rAk_d, rBk_d))):
        for c in range(8):
            P.emit("gpsimd", lambda e, c=c, w_d=w_d: e.dma_start(
                wbuf[:, c, :], w_d[128 * c:128 * (c + 1), :]), inc=("lG", 16))
        P.emit("sync", lambda e, tA_d=tA_d: e.dma_start(tA[:, :], tA_d[:, :]),
               inc=("dS", 16))
        P.emit("sync", lambda e, tB_d=tB_d: e.dma_start(tB[:, :], tB_d[:, :]),
               inc=("dS", 16))
        lwv = P.val("lG")
        ldv = P.val("dS")
        rqw = Ring(P, "rqw", 2, ca=1)
        for fc in range(8):
            for tg in range(8):
                sl, w = rp.acquire()
                if fc == 0 and tg == 0:
                    w = w + [("lG", lwv), ("dS", ldv)]
                for ec in range(8):
                    P.emit("tensor", lambda e, fc=fc, tg=tg, ec=ec, sl=sl: e.matmul(
                        R.mm_slots[sl][:, :], wbuf[:, ec, 128 * fc:128 * (fc + 1)],
                        lnT[:, ec, 512 * tg:512 * (tg + 1)],
                        start=(ec == 0), stop=(ec == 7)),
                        waits=(w if ec == 0 else []),
                        inc=(rp.fill_inc(1) if ec == 7 else None))
                rp.commit()
                sqw, wqw = rqw.acquire()
                d = rp.done()
                P.emit("scalar", lambda e, sl=sl, sqw=sqw: e.copy(
                    qwork[:, sqw, :], R.mm_slots[sl][:, :]),
                    waits=rp.cons(idx=rp.drains - 1) + wqw, inc=d)
                vq = P.val(d[0])
                rqw.commit()
                sl2, w2 = rp.acquire()
                P.emit("tensor", lambda e, sl2=sl2, sqw=sqw: e.matmul(
                    R.mm_slots[sl2][:, :], rsgn[:, :], qwork[:, sqw, :],
                    start=True, stop=True),
                    waits=w2 + [(d[0], vq), ("dG", P.val("dG"))],
                    inc=rp.fill_inc(1))
                rp.commit()
                d2 = rp.done()
                P.emit("vector", lambda e, sl2=sl2, tg=tg: e.scalar_tensor_tensor(
                    tmpf[:, :], R.mm_slots[sl2][:, :], 1.0,
                    tB[:, 512 * tg:512 * (tg + 1)], op0=ALU.mult, op1=ALU.mult),
                    waits=rp.cons(idx=rp.drains - 1), inc=d2)
                P.emit("vector", lambda e, sqw=sqw, tg=tg: e.tensor_tensor(
                    tmpg[:, :], qwork[:, sqw, :], tA[:, 512 * tg:512 * (tg + 1)],
                    op=ALU.mult), inc=rqw.done())
                P.emit("vector", lambda e, tg=tg: e.tensor_add(
                    qchunk[:, 512 * tg:512 * (tg + 1)], tmpf[:, :], tmpg[:, :]),
                    waits=(P.flow("dQ") if tg == 0 else []), inc=("lv", 1))
            P.emit("sync", lambda e, fc=fc, sp=sp: e.dma_start(
                sp[128 * fc:128 * (fc + 1), :], qchunk[:, :]),
                waits=P.flow("lv"), inc=("dQ", 16))
        P.barrier(name="bar")

    ctx_all = a.at(m0, [128, ND, T], BF16)
    a.reset(mQ)
    P.emit("sync", lambda e: None, waits=P.flow("dQ"))
    emit_attn_core(P, R, qsp, ksp, Vall, ctx_all, nheads=16, dh_qk=64, dh_v=64,
                   causal=True)
    P.barrier(name="bar")

    rp = Ring(P, "mm", 4, ca=1)
    wbuf = a.alloc([128, 8, 1024], BF16)
    xres = a.alloc([128, 2, D], F32)
    for c in range(8):
        P.emit("gpsimd", lambda e, c=c: e.dma_start(
            wbuf[:, c, :], ow_d[128 * c:128 * (c + 1), :]), inc=("lG", 16))
    lwv = P.val("lG")
    rxr = Ring(P, "rxr", 2, ca=16)
    for tt in range(NT):
        sx, wx = rxr.acquire()
        P.emit("sync", lambda e, tt=tt, sx=sx: e.dma_start(
            xres[:, sx, :], x_in[128 * tt:128 * (tt + 1), :]),
            waits=wx, inc=rxr.fill_inc(16))
        rxr.commit()
        stt_w = []
        for hf in range(2):
            sl, w = rp.acquire()
            if tt == 0 and hf == 0:
                w = w + [("lG", lwv)]
            for dc in range(8):
                P.emit("tensor", lambda e, tt=tt, hf=hf, dc=dc, sl=sl: e.matmul(
                    R.mm_slots[sl][:, :], ctx_all[:, dc, 128 * tt:128 * (tt + 1)],
                    wbuf[:, dc, 512 * hf:512 * (hf + 1)],
                    start=(dc == 0), stop=(dc == 7)),
                    waits=(w if dc == 0 else []),
                    inc=(rp.fill_inc(1) if dc == 7 else None))
            rp.commit()
            wv = rp.cons() + (rxr.cons() if hf == 0 else [])
            csl = rp.cslot()
            dd = rp.done()
            v = P.emit("vector", lambda e, hf=hf, sl=csl, sx=rxr.cslot():
                       e.scalar_tensor_tensor(
                           xres[:, sx, 512 * hf:512 * (hf + 1)], R.mm_slots[sl][:, :],
                           1.0, xres[:, sx, 512 * hf:512 * (hf + 1)],
                           op0=ALU.mult, op1=ALU.add),
                       waits=wv, inc=dd)
            stt_w.append((dd[0], v))
        P.emit("sync", lambda e, tt=tt, sx=rxr.cslot(): e.dma_start(
            x_out[128 * tt:128 * (tt + 1), :], xres[:, sx, :]),
            waits=stt_w, inc=rxr.done())
    P.emit("gpsimd", lambda e: None, waits=rxr.all_drained_waits())
    a.reset(m0)


def build_L1_attn(P, R, x_in, x_out, wqd_d, wkd_d, wvd_d, upqc_d, upqe_d,
                  upkc_d, upke_d, upv_d, wc_d, tab_d, rsgn32_d):
    a = R.arena
    m0 = a.mark()
    lnT = a.alloc([128, ND, T], BF16)
    Vall = a.alloc([128, 32, 16, 65], BF16)
    mQ = a.mark()
    wbuf = a.alloc([128, 8, 1024], BF16)
    qchunk = a.alloc([128, T], BF16)
    xres = None
    qsp = P.dram("l1a_qsp", [D, T], BF16)
    ksp = P.dram("l1a_ksp", [D, T], BF16)
    vsp = P.dram("l1a_vsp", [D, T], BF16)
    ups = P.sbuf("l1a_ups", [64, 8, 32], BF16)
    rsg = P.sbuf("l1a_rsg", [32, 32], BF16)

    def consume(i, lt, s2, rl):
        pairs = [(lt[:, s2, 128 * c:128 * (c + 1)],
                  lnT[:, c, 128 * i:128 * (i + 1)]) for c in range(ND)]
        emit_transposes(P, R, pairs, extra_waits=rl.cons())
        P.emit("vector", lambda e: None, inc=rl.done())

    emit_ln_tiles(P, R, x_in, 0, T, consume)
    for i, dsrc in enumerate((upqc_d, upqe_d, upkc_d, upke_d)):
        P.emit("gpsimd", lambda e, i=i, dsrc=dsrc: e.dma_start(
            ups[:, (0, 1, 3, 4)[i], :], dsrc[:, :]), inc=("dG", 16))
    P.emit("gpsimd", lambda e: e.dma_start(
        ups[:, 6:8, :].rearrange("p a b -> p (a b)"), upv_d[:, :]), inc=("dG", 16))

    P.emit("gpsimd", lambda e: e.dma_start(rsg[:, :], rsgn32_d[:, :]), inc=("dG", 16))
    P.emit("gpsimd", lambda e: None, waits=P.flow("dG"))
    P.barrier(name="bar")
    for j, src_col in ((2, 1), (5, 4)):
        P.emit("tensor", lambda e, src_col=src_col: e.transpose(
            R.psT[0:32, 0:64], ups[:, src_col, :], R.eye_bf[0:64, 0:64]),
            waits=P.flow("lv") + P.flow("ls"), inc=("lt", 1))
        P.emit("vector", lambda e: e.tensor_copy(qchunk[0:32, 0:64],
                                                 R.psT[0:32, 0:64]),
               waits=P.flow("lt"), inc=("lv", 1))
        P.emit("tensor", lambda e: e.matmul(
            R.psM[0:32, 64:128], rsg[:, :], qchunk[0:32, 0:64],
            start=True, stop=True), waits=P.flow("lv"), inc=("lt", 1))
        P.emit("vector", lambda e: e.tensor_copy(qchunk[0:32, 64:128],
                                                 R.psM[0:32, 64:128]),
               waits=P.flow("lt"), inc=("lv", 1))
        P.emit("tensor", lambda e: e.transpose(
            R.psT[0:64, 128:160], qchunk[0:32, 64:128],
            R.eye_bf[0:32, 0:32]), waits=P.flow("lv"), inc=("lt", 1))
        P.emit("vector", lambda e, j=j: e.tensor_copy(ups[:, j, :], R.psT[0:64, 128:160]),
               waits=P.flow("lt"), inc=("lv", 1))
    P.emit("vector", lambda e: e.tensor_scalar_mul(
        ups[:, 3:6, :].rearrange("p a b -> p (a b)"),
        ups[:, 3:6, :].rearrange("p a b -> p (a b)"), 0.125),
        waits=P.flow("lv"), inc=("lv", 1))
    P.emit("vector", lambda e: e.memset(Vall[:, :, :, 64:65], 1.0), inc=("lv", 1))
    P.emit("vector", lambda e: None, waits=P.flow("lv"), inc=("Vrdy", 1))

    rp = Ring(P, "mm", 4, ca=1)
    rqw = Ring(P, "rqw", 2, ca=1)
    for w_d, sp in ((wqd_d, qsp), (wkd_d, ksp), (wvd_d, vsp)):
        for c in range(8):
            P.emit("gpsimd", lambda e, c=c, w_d=w_d: e.dma_start(
                wbuf[:, c, :], w_d[128 * c:128 * (c + 1), :]), inc=("lG", 16))
        lwv = P.val("lG")
        for fc in range(8):
            for tg in range(8):
                sl, w = rp.acquire()
                if fc == 0 and tg == 0:
                    w = w + [("lG", lwv)]
                for ec in range(8):
                    P.emit("tensor", lambda e, fc=fc, tg=tg, ec=ec, sl=sl: e.matmul(
                        R.mm_slots[sl][:, :], wbuf[:, ec, 128 * fc:128 * (fc + 1)],
                        lnT[:, ec, 512 * tg:512 * (tg + 1)],
                        start=(ec == 0), stop=(ec == 7)),
                        waits=(w if ec == 0 else []),
                        inc=(rp.fill_inc(1) if ec == 7 else None))
                rp.commit()
                d = rp.done()
                P.emit("scalar", lambda e, tg=tg, sl=sl: e.copy(
                    qchunk[:, 512 * tg:512 * (tg + 1)], R.mm_slots[sl][:, :]),
                    waits=rp.cons(idx=rp.drains - 1) +
                    (P.flow("dQ") if tg == 0 else []), inc=d)
            P.emit("sync", lambda e, fc=fc, sp=sp: e.dma_start(
                sp[128 * fc:128 * (fc + 1), :], qchunk[:, :]),
                waits=[(rp.cs[i], P.val(rp.cs[i])) for i in range(4)],
                inc=("dQ", 16))
        P.barrier(name="bar")

    ctx_all = a.at(m0, [128, ND, T], BF16)
    a.reset(mQ)
    lat = a.alloc([64, 1, 3, 512], BF16)
    tabs = a.alloc([32, 1, 2, 512], BF16)
    tmpa = a.alloc([32, 512], F32)
    tmpb = a.alloc([32, 512], F32)
    vstage = a.alloc([64, T], BF16)
    P.emit("sync", lambda e: None, waits=P.flow("dQ"))

    def qk_load(h, sq, wq, qh, kh, rqk, rp2):
        rlat = Ring(P, "rlat", 1, ca=1)
        for tg in range(8):
            slat, wlat = rlat.acquire()
            if tg == 0:
                wlat = wlat + wq
            P.emit("sync", lambda e, h=h, tg=tg, slat=slat: e.dma_start(
                lat[:, slat, 0, :], qsp[64 * h:64 * (h + 1), 512 * tg:512 * (tg + 1)]),
                waits=wlat, inc=rlat.fill_inc(16))
            P.emit("sync", lambda e, h=h, tg=tg, slat=slat: e.dma_start(
                lat[:, slat, 1, :], ksp[64 * h:64 * (h + 1), 512 * tg:512 * (tg + 1)]),
                inc=rlat.fill_inc(16))
            P.emit("sync", lambda e, h=h, tg=tg, slat=slat: e.dma_start(
                lat[:, slat, 2, :], vsp[64 * h:64 * (h + 1), 512 * tg:512 * (tg + 1)]),
                inc=rlat.fill_inc(16))
            P.emit("sync", lambda e, tg=tg, slat=slat: e.dma_start(
                tabs[:, slat, 0, :], tab_d[:, 512 * tg:512 * (tg + 1)]),
                inc=rlat.fill_inc(16))
            P.emit("sync", lambda e, tg=tg, slat=slat: e.dma_start(
                tabs[:, slat, 1, :], tab_d[:, T + 512 * tg:T + 512 * (tg + 1)]),
                inc=rlat.fill_inc(16))
            rlat.commit()
            wL = rlat.cons()
            for which, dst in ((0, qh), (1, kh)):
                sl, w = rp2.acquire()
                if which == 0:
                    w = w + wL
                P.emit("tensor", lambda e, tg=tg, which=which, sl=sl, slat=rlat.cslot():
                       e.matmul(
                    R.mm_slots[sl][0:64, :],
                    ups[:, 3 * which:3 * which + 2, :].rearrange("p a b -> p (a b)"),
                    lat[:, slat, which, :], start=True, stop=True),
                    waits=w, inc=rp2.fill_inc(1))
                rp2.commit()
                sl2, w2 = rp2.acquire()
                P.emit("tensor", lambda e, tg=tg, which=which, sl2=sl2, slat=rlat.cslot():
                       e.matmul(
                    R.mm_slots[sl2][0:32, :], ups[:, 3 * which + 2, :],
                    lat[:, slat, which, :], start=True, stop=True),
                    waits=w2, inc=rp2.fill_inc(1))
                rp2.commit()
                d = rp2.done()
                P.emit("scalar", lambda e, tg=tg, sl=sl, dst=dst: e.copy(
                    dst[0:32, 0, 512 * tg:512 * (tg + 1)], R.mm_slots[sl][0:32, :]),
                    waits=rp2.cons(idx=rp2.drains - 1), inc=d)
                P.emit("vector", lambda e, tg=tg, sl=sl, slat=rlat.cslot():
                       e.scalar_tensor_tensor(
                    tmpa[:, :], R.mm_slots[sl][32:64, :], 1.0,
                    tabs[:, slat, 0, :], op0=ALU.mult, op1=ALU.mult),
                    waits=rp2.cons(idx=rp2.drains - 1) + P.flow("lv"), inc=("lv", 1))
                d2 = rp2.done()
                P.emit("vector", lambda e, tg=tg, sl2=sl2, slat=rlat.cslot():
                       e.scalar_tensor_tensor(
                    tmpb[:, :], R.mm_slots[sl2][0:32, :], 1.0,
                    tabs[:, slat, 1, :], op0=ALU.mult, op1=ALU.mult),
                    waits=rp2.cons(idx=rp2.drains - 1), inc=d2)
                P.emit("vector", lambda e, tg=tg, dst=dst: e.tensor_add(
                    dst[32:64, 0, 512 * tg:512 * (tg + 1)], tmpa[:, :], tmpb[:, :]),
                    inc=rqk.fill_inc(1))
            sl, w = rp2.acquire()
            P.emit("tensor", lambda e, tg=tg, sl=sl, slat=rlat.cslot(): e.matmul(
                R.mm_slots[sl][0:64, :],
                ups[:, 6:8, :].rearrange("p a b -> p (a b)"),
                lat[:, slat, 2, :], start=True, stop=True),
                waits=w, inc=rp2.fill_inc(1))
            rp2.commit()
            d = rp2.done()
            v = P.emit("scalar", lambda e, tg=tg, sl=sl: e.copy(
                vstage[:, 512 * tg:512 * (tg + 1)],
                R.mm_slots[sl][0:64, :]),
                waits=rp2.cons(idx=rp2.drains - 1), inc=d)
            P.emit("vector", lambda e: None, waits=[(d[0], v)], inc=rlat.done())
        vT = vstage
        rt = Ring(P, "trp", 2, ca=1)
        for g in range(0, NT, 8):
            s, wg = rt.acquire()
            bank = R.psT if s == 0 else R.psT2
            if g == 0:
                wg = wg + P.flow("ls")
            for k in range(8):
                c = g + k
                P.emit("tensor", lambda e, c=c, k=k, bank=bank: e.transpose(
                    bank[:, 64 * k:64 * (k + 1)],
                    vT[:, 128 * c:128 * (c + 1)], R.eye_bf[0:64, 0:64]),
                    waits=(wg if k == 0 else []),
                    inc=(rt.fill_inc(1) if k == 7 else None))
            rt.commit()
            wv2 = rt.cons()
            for k in range(8):
                c = g + k
                P.emit("vector", lambda e, c=c, k=k, h=h, bank=bank: e.tensor_copy(
                    Vall[:, c, h, 0:64], bank[:, 64 * k:64 * k + 64]),
                    waits=(wv2 if k == 0 else []),
                    inc=(rt.done() if k == 7 else None))
        P.emit("vector", lambda e: None, waits=rt.all_drained_waits(), inc=rqk.fill_inc(1))

    emit_attn_core(P, R, None, None, Vall, ctx_all, nheads=16, dh_qk=64, dh_v=64,
                   causal=True, qk_load=qk_load)
    P.barrier(name="bar")

    rp = Ring(P, "mm", 4, ca=1)
    wbuf2 = a.alloc([128, 8, 1024], BF16)
    xres = a.alloc([128, 2, D], F32)
    for c in range(8):
        P.emit("gpsimd", lambda e, c=c: e.dma_start(
            wbuf2[:, c, :], wc_d[128 * c:128 * (c + 1), :]), inc=("lG", 16))
    lwv = P.val("lG")
    rxr = Ring(P, "rxr", 2, ca=16)
    for tt in range(NT):
        sx, wx = rxr.acquire()
        P.emit("sync", lambda e, tt=tt, sx=sx: e.dma_start(
            xres[:, sx, :], x_in[128 * tt:128 * (tt + 1), :]),
            waits=wx, inc=rxr.fill_inc(16))
        rxr.commit()
        stt_w = []
        for hf in range(2):
            sl, w = rp.acquire()
            if tt == 0 and hf == 0:
                w = w + [("lG", lwv)]
            for dc in range(8):
                P.emit("tensor", lambda e, tt=tt, hf=hf, dc=dc, sl=sl: e.matmul(
                    R.mm_slots[sl][:, :], ctx_all[:, dc, 128 * tt:128 * (tt + 1)],
                    wbuf2[:, dc, 512 * hf:512 * (hf + 1)],
                    start=(dc == 0), stop=(dc == 7)),
                    waits=(w if dc == 0 else []),
                    inc=(rp.fill_inc(1) if dc == 7 else None))
            rp.commit()
            wv = rp.cons() + (rxr.cons() if hf == 0 else [])
            csl = rp.cslot()
            dd = rp.done()
            v = P.emit("vector", lambda e, hf=hf, sl=csl, sx=rxr.cslot():
                       e.scalar_tensor_tensor(
                           xres[:, sx, 512 * hf:512 * (hf + 1)], R.mm_slots[sl][:, :],
                           1.0, xres[:, sx, 512 * hf:512 * (hf + 1)],
                           op0=ALU.mult, op1=ALU.add),
                       waits=wv, inc=dd)
            stt_w.append((dd[0], v))
        P.emit("sync", lambda e, tt=tt, sx=rxr.cslot(): e.dma_start(
            x_out[128 * tt:128 * (tt + 1), :], xres[:, sx, :]),
            waits=stt_w, inc=rxr.done())
    P.emit("gpsimd", lambda e: None, waits=rxr.all_drained_waits())
    a.reset(m0)


def build_L3_attn(P, R, x_in, x_out, wq_d, wkv_d, wk_d, wv_d, wo_d, qg_d, kg_d,
                  sel2_d, sel2T_d, sel4_d, sel4T_d):
    a = R.arena
    m0 = a.mark()
    lnT = a.alloc([128, ND, T], BF16)
    Vall = a.alloc([128, 32, 16, 33], BF16)
    ctxz = a.mark()
    mQ = a.mark()
    wbuf = a.alloc([128, 8, 1024], BF16)
    kvT = a.alloc([128, T], BF16)
    qchunk = a.alloc([128, T], BF16)
    qT_sb = a.alloc([128, 512], BF16)
    sq_sb = a.alloc([128, 512], F32)
    rstd = a.alloc([4, 2, 512], F32)
    rstd_bf = a.alloc([4, 512], BF16)
    sel2 = P.sbuf("l3a_sel2_sb", [128, 2], F32)
    sel2T = P.sbuf("l3a_sel2T_sb", [2, 128], BF16)
    sel4 = P.sbuf("l3a_sel4_sb", [128, 4], F32)
    sel4T = P.sbuf("l3a_sel4T_sb", [4, 128], BF16)
    qsp = P.dram("l3a_qsp", [D, T], BF16)
    ksp = P.dram("l3a_ksp", [512, T], BF16)

    def consume(i, lt, s2, rl):
        pairs = [(lt[:, s2, 128 * c:128 * (c + 1)],
                  lnT[:, c, 128 * i:128 * (i + 1)]) for c in range(ND)]
        emit_transposes(P, R, pairs, extra_waits=rl.cons())
        P.emit("vector", lambda e: None, inc=rl.done())

    emit_ln_tiles(P, R, x_in, 0, T, consume)
    for nm, dsrc in (("l3a_sel2_sb", sel2_d), ("l3a_sel2T_sb", sel2T_d),
                     ("l3a_sel4_sb", sel4_d), ("l3a_sel4T_sb", sel4T_d)):
        P.emit("gpsimd", lambda e, nm=nm, dsrc=dsrc: e.dma_start(
            P.tensors[nm].ap(), dsrc[:, :]), inc=("dG", 16))
    P.emit("gpsimd", lambda e: None, waits=P.flow("dG"))
    P.barrier(name="bar")

    qg2row = P.sbuf("l3a_qg2", [2, 128], BF16)
    kg4row = P.sbuf("l3a_kg4", [4, 128], BF16)
    sel2g = P.sbuf("l3a_sel2g", [2, 128], BF16)
    sel4g = P.sbuf("l3a_sel4g", [4, 128], BF16)
    for j in range(2):
        for r in range(2):
            P.emit("gpsimd", lambda e, j=j, r=r: e.dma_start(
                qg2row[j:j + 1, 64 * r:64 * (r + 1)], qg_d), inc=("dG", 16))
    for j in range(4):
        for r in range(4):
            P.emit("gpsimd", lambda e, j=j, r=r: e.dma_start(
                kg4row[j:j + 1, 32 * r:32 * (r + 1)], kg_d), inc=("dG", 16))
    P.emit("vector", lambda e: e.tensor_mul(sel2g[:, :], sel2T[:, :], qg2row[:, :]),
           waits=P.flow("dG") + [("dG", P.val("dG"))], inc=("lv", 1))
    P.emit("vector", lambda e: e.tensor_mul(sel4g[:, :], sel4T[:, :], kg4row[:, :]),
           inc=("lv", 1))
    P.emit("vector", lambda e: e.tensor_scalar_mul(sel4g[:, :], sel4g[:, :], 0.125),
           waits=P.flow("lv"), inc=("lv", 1))
    P.emit("vector", lambda e: e.memset(Vall[:, :, :, 32:33], 1.0), inc=("lv", 1))
    P.emit("vector", lambda e: None, waits=P.flow("lv"), inc=("Vrdy", 1))

    rp = Ring(P, "mm", 4, ca=1)

    P.emit("sync", lambda e: e.dma_start(
        wbuf[:, 0:1, 0:128].rearrange("p a b -> p (a b)"), wkv_d[0:128, :]
        ) if False else None)
    for c in range(8):
        P.emit("gpsimd", lambda e, c=c: e.dma_start(
            wbuf[:, c, 0:128], wkv_d[128 * c:128 * (c + 1), :]), inc=("lG", 16))
    lwv = P.val("lG")
    for tg in range(8):
        sl, w = rp.acquire()
        if tg == 0:
            w = w + [("lG", lwv)]
        for ec in range(8):
            P.emit("tensor", lambda e, tg=tg, ec=ec, sl=sl: e.matmul(
                R.mm_slots[sl][:, :], wbuf[:, ec, 0:128],
                lnT[:, ec, 512 * tg:512 * (tg + 1)],
                start=(ec == 0), stop=(ec == 7)),
                waits=(w if ec == 0 else []),
                inc=(rp.fill_inc(1) if ec == 7 else None))
        rp.commit()
        d = rp.done()
        P.emit("scalar", lambda e, tg=tg, sl=sl: e.copy(
            kvT[:, 512 * tg:512 * (tg + 1)], R.mm_slots[sl][:, :]),
            waits=rp.cons(idx=rp.drains - 1), inc=d)
    P.barrier(name="bar")

    P.emit("gpsimd", lambda e: e.dma_start(
        wbuf[:, 0, 0:512], wk_d[:, :]), inc=("lG", 16))
    lwv = P.val("lG")
    for fc in range(4):
        for tg in range(8):
            sl, w = rp.acquire()
            if fc == 0 and tg == 0:
                w = w + [("lG", lwv)]
            P.emit("tensor", lambda e, fc=fc, tg=tg, sl=sl: e.matmul(
                R.mm_slots[sl][:, :], wbuf[:, 0, 128 * fc:128 * (fc + 1)],
                kvT[:, 512 * tg:512 * (tg + 1)], start=True, stop=True),
                waits=w, inc=rp.fill_inc(1))
            rp.commit()
            d = rp.done()
            P.emit("scalar", lambda e, sl=sl: e.copy(qT_sb[:, :], R.mm_slots[sl][:, :]),
                   waits=rp.cons(idx=rp.drains - 1) + P.flow("lv2"), inc=("ls", 1))
            P.emit("scalar", lambda e, sl=sl: e.activation(
                sq_sb[:, :], R.mm_slots[sl][:, :], AF.Square), inc=d)
            P.emit("tensor", lambda e: e.matmul(
                R.psM[0:4, 0:512], sel4[:, :], sq_sb[:, :], start=True, stop=True),
                waits=P.flow("ls") + [(d[0], P.val(d[0]))] + P.flow("lv2"), inc=("lt", 1))
            P.emit("vector", lambda e: e.tensor_scalar_mul(rstd[:, 0, :],
                                                           R.psM[0:4, 0:512], 1.0 / 32),
                   waits=P.flow("lt"), inc=("lv", 1))
            P.emit("vector", lambda e: e.tensor_scalar_add(rstd[:, 0, :], rstd[:, 0, :],
                                                           1e-6),
                   waits=P.flow("lv"), inc=("lv", 1))
            P.emit("scalar", lambda e: e.activation(rstd[:, 1, :], rstd[:, 0, :],
                                                    AF.Sqrt),
                   waits=P.flow("lv"), inc=("ls", 1))
            def _re(e):
                with P.nc.allow_low_precision(reason="rstd bcast operand bf16"):
                    return e.reciprocal(rstd_bf[:, :], rstd[:, 1, :])
            P.emit("vector", _re, waits=P.flow("ls"), inc=("lv", 1))
            P.emit("tensor", lambda e: e.matmul(
                R.psM[:, 0:512], sel4g[:, :], rstd_bf[:, :], start=True, stop=True),
                waits=P.flow("lv"), inc=("lt", 1))
            v2 = P.emit("vector", lambda e, tg=tg: e.scalar_tensor_tensor(
                qchunk[:, 512 * tg:512 * (tg + 1)], R.psM[:, 0:512], 1.0,
                qT_sb[:, :], op0=ALU.mult, op1=ALU.mult),
                waits=P.flow("lt"), inc=("lv2", 1))
        P.emit("sync", lambda e, fc=fc: e.dma_start(
            ksp[128 * fc:128 * (fc + 1), :], qchunk[:, :]),
            waits=P.flow("lv2"), inc=("dQ", 16))
        P.emit("vector", lambda e: None, waits=P.flow("dQ"), inc=("lv", 1))
    P.barrier(name="bar")

    for c in range(8):
        P.emit("gpsimd", lambda e, c=c: e.dma_start(
            wbuf[:, c, :], wq_d[128 * c:128 * (c + 1), :]), inc=("lG", 16))
    lwv = P.val("lG")
    for fc in range(8):
        for tg in range(8):
            sl, w = rp.acquire()
            if fc == 0 and tg == 0:
                w = w + [("lG", lwv)]
            for ec in range(8):
                P.emit("tensor", lambda e, fc=fc, tg=tg, ec=ec, sl=sl: e.matmul(
                    R.mm_slots[sl][:, :], wbuf[:, ec, 128 * fc:128 * (fc + 1)],
                    lnT[:, ec, 512 * tg:512 * (tg + 1)],
                    start=(ec == 0), stop=(ec == 7)),
                    waits=(w if ec == 0 else []),
                    inc=(rp.fill_inc(1) if ec == 7 else None))
            rp.commit()
            d = rp.done()
            P.emit("scalar", lambda e, sl=sl: e.copy(qT_sb[:, :], R.mm_slots[sl][:, :]),
                   waits=rp.cons(idx=rp.drains - 1) + P.flow("lv2"), inc=("ls", 1))
            P.emit("scalar", lambda e, sl=sl: e.activation(
                sq_sb[:, :], R.mm_slots[sl][:, :], AF.Square), inc=d)
            P.emit("tensor", lambda e: e.matmul(
                R.psM[0:2, 0:512], sel2[:, :], sq_sb[:, :], start=True, stop=True),
                waits=P.flow("ls") + [(d[0], P.val(d[0]))] + P.flow("lv2"), inc=("lt", 1))
            P.emit("vector", lambda e: e.tensor_scalar_mul(rstd[0:2, 0, :],
                                                           R.psM[0:2, 0:512], 1.0 / 64),
                   waits=P.flow("lt"), inc=("lv", 1))
            P.emit("vector", lambda e: e.tensor_scalar_add(rstd[0:2, 0, :],
                                                           rstd[0:2, 0, :], 1e-6),
                   waits=P.flow("lv"), inc=("lv", 1))
            P.emit("scalar", lambda e: e.activation(rstd[0:2, 1, :], rstd[0:2, 0, :],
                                                    AF.Sqrt),
                   waits=P.flow("lv"), inc=("ls", 1))
            def _re2(e):
                with P.nc.allow_low_precision(reason="rstd bcast operand bf16"):
                    return e.reciprocal(rstd_bf[0:2, :], rstd[0:2, 1, :])
            P.emit("vector", _re2, waits=P.flow("ls"), inc=("lv", 1))
            P.emit("tensor", lambda e: e.matmul(
                R.psM[:, 0:512], sel2g[:, :], rstd_bf[0:2, :], start=True, stop=True),
                waits=P.flow("lv"), inc=("lt", 1))
            v2 = P.emit("vector", lambda e, tg=tg: e.scalar_tensor_tensor(
                qchunk[:, 512 * tg:512 * (tg + 1)], R.psM[:, 0:512], 1.0,
                qT_sb[:, :], op0=ALU.mult, op1=ALU.mult),
                waits=P.flow("lt"), inc=("lv2", 1))
        P.emit("sync", lambda e, fc=fc: e.dma_start(
            qsp[128 * fc:128 * (fc + 1), :], qchunk[:, :]),
            waits=P.flow("lv2"), inc=("dQ", 16))
        P.emit("vector", lambda e: None, waits=P.flow("dQ"), inc=("lv", 1))
    P.barrier(name="bar")

    P.emit("gpsimd", lambda e: e.dma_start(
        wbuf[:, 0, 0:512], wv_d[:, :]), inc=("lG", 16))
    lwv = P.val("lG")
    for tt in range(NT):
        sl, w = rp.acquire()
        if tt == 0:
            w = w + [("lG", lwv)]
        P.emit("tensor", lambda e, tt=tt, sl=sl: e.matmul(
            R.mm_slots[sl][:, :], kvT[:, 128 * tt:128 * (tt + 1)],
            wbuf[:, 0, 0:512], start=True, stop=True),
            waits=w, inc=rp.fill_inc(1))
        rp.commit()
        d = rp.done()
        P.emit("vector", lambda e, tt=tt, sl=sl: e.tensor_copy(
            Vall[:, tt, :, 0:32], R.mm_slots[sl][:, :]),
            waits=rp.cons(idx=rp.drains - 1), inc=d)
    P.barrier(name="bar")

    ctx_all = a.at(m0, [128, 4, T], BF16)
    a.reset(mQ)
    P.emit("sync", lambda e: None, waits=P.flow("dQ"))

    def qk_load(h, sq, wq, qh, kh, rqk, rp2):
        P.emit("sync", lambda e, h=h: e.dma_start(
            qh[:, sq, :], qsp[64 * h:64 * h + 32, :]),
            waits=wq, inc=rqk.fill_inc(16))
        P.emit("sync", lambda e, h=h: e.dma_start(
            kh[:, sq, :], ksp[32 * h:32 * (h + 1), :]),
            inc=rqk.fill_inc(16))

    def ctx_slice(h, i0):
        return ctx_all[32 * (h % 4):32 * (h % 4) + 32, h // 4, i0:i0 + 512]

    emit_attn_core(P, R, None, None, Vall, ctx_all, nheads=16, dh_qk=32, dh_v=32,
                   causal=True, qk_load=qk_load, ctx_slice=ctx_slice)
    P.barrier(name="bar")

    rp = Ring(P, "mm", 4, ca=1)
    wbuf2 = a.alloc([128, 4, 1024], BF16)
    xres = a.alloc([128, 2, D], F32)
    for c in range(4):
        src = bass.AP(P.tensors["L3_attn_Wo"], 4 * c * 64 * 1024,
                      [[64 * 1024, 4], [1024, 32], [1, 1024]])
        P.emit("gpsimd", lambda e, c=c, src=src: e.dma_start(
            wbuf2[:, c, :], src), inc=("lG", 16))
    lwv = P.val("lG")
    rxr = Ring(P, "rxr", 2, ca=16)
    for tt in range(NT):
        sx, wx = rxr.acquire()
        P.emit("sync", lambda e, tt=tt, sx=sx: e.dma_start(
            xres[:, sx, :], x_in[128 * tt:128 * (tt + 1), :]),
            waits=wx, inc=rxr.fill_inc(16))
        rxr.commit()
        stt_w = []
        for hf in range(2):
            sl, w = rp.acquire()
            if tt == 0 and hf == 0:
                w = w + [("lG", lwv)]
            for dc in range(4):
                P.emit("tensor", lambda e, tt=tt, hf=hf, dc=dc, sl=sl: e.matmul(
                    R.mm_slots[sl][:, :], ctx_all[:, dc, 128 * tt:128 * (tt + 1)],
                    wbuf2[:, dc, 512 * hf:512 * (hf + 1)],
                    start=(dc == 0), stop=(dc == 3)),
                    waits=(w if dc == 0 else []),
                    inc=(rp.fill_inc(1) if dc == 3 else None))
            rp.commit()
            wv = rp.cons() + (rxr.cons() if hf == 0 else [])
            csl = rp.cslot()
            dd = rp.done()
            v = P.emit("vector", lambda e, hf=hf, sl=csl, sx=rxr.cslot():
                       e.scalar_tensor_tensor(
                           xres[:, sx, 512 * hf:512 * (hf + 1)], R.mm_slots[sl][:, :],
                           1.0, xres[:, sx, 512 * hf:512 * (hf + 1)],
                           op0=ALU.mult, op1=ALU.add),
                       waits=wv, inc=dd)
            stt_w.append((dd[0], v))
        P.emit("sync", lambda e, tt=tt, sx=rxr.cslot(): e.dma_start(
            x_out[128 * tt:128 * (tt + 1), :], xres[:, sx, :]),
            waits=stt_w, inc=rxr.done())
    P.emit("gpsimd", lambda e: None, waits=rxr.all_drained_waits())
    a.reset(m0)


def build_L0_attn(P, R, x_in, x_out, wqkv_d, bqkv_d, wout_d, bout_d):
    a = R.arena
    m0 = a.mark()
    lnT = a.alloc([128, ND, T], BF16)
    ctx3 = a.alloc([128, ND, T], BF16)
    Vh = a.alloc([128, 32, 1, 65], BF16)
    qhb = a.alloc([64, T], BF16)
    khb = a.alloc([64, T], BF16)
    vhb = a.alloc([64, T], BF16)
    wqr = a.alloc([128, 2, 8, 192], BF16)
    bcol = a.alloc([128, 48], F32)
    bout_row = a.alloc([1, D], BF16)

    wqkvT, wq_dw = emit_prepass_transpose(P, R, wqkv_d, 3072, D, "l0aqkv")
    woutT, wo_dw = emit_prepass_transpose(P, R, wout_d, D, D, "l0aout")

    def consume(i, lt, s2, rl):
        pairs = [(lt[:, s2, 128 * c:128 * (c + 1)],
                  lnT[:, c, 128 * i:128 * (i + 1)]) for c in range(ND)]
        emit_transposes(P, R, pairs, extra_waits=rl.cons())
        P.emit("vector", lambda e: None, inc=rl.done())

    emit_ln_tiles(P, R, x_in, 0, T, consume)
    P.barrier(name="bar")
    emit_colstage(P, R, bqkv_d, 3072, bcol[:, 0:24])
    P.emit("vector", lambda e: e.tensor_scalar_mul(bcol[:, 24:48], bcol[:, 0:24],
                                                   0.125),
           waits=P.flow("lv"), inc=("lv", 1))
    P.emit("gpsimd", lambda e: e.dma_start(bout_row[0:1, :], bout_d), inc=("dG", 16))
    P.emit("vector", lambda e: e.memset(Vh[:, :, :, 64:65], 1.0), inc=("lv", 1))
    P.emit("vector", lambda e: None, waits=P.flow("lv"), inc=("Vrdy", 1))
    P.emit("gpsimd", lambda e: None, waits=P.flow("dG"))
    P.barrier(name="bar")

    def qk_load(h, sq, wq, qh, kh, rqk, rp2):
        rw2 = Ring(P, "rw2", 2, ca=1)
        for w in range(16):
            sw, ww = rw2.acquire()
            src = bass.AP(P.tensors["l0aqkv_T"], 192 * w,
                          [[3072, 128], [3072 * 128, 8], [1, 192]])
            wld = list(ww) + (wq_dw if h == 0 and w == 0 else []) + \
                (list(wq) if w == 0 else [])
            P.emit("sync", lambda e, src=src, sw=sw: e.dma_start(
                wqr[:, sw, :, :], src), waits=wld, inc=rw2.fill_inc(16))
            rw2.commit()
            wcons = rw2.cons()
            vv = None
            for part, (dst, c0, bj) in enumerate(((qhb, 0, 0), (khb, 64, 1),
                                                  (vhb, 128, 0))):
                sl, wsl = rp2.acquire()
                if part == 0:
                    wsl = wsl + wcons
                for ec in range(8):
                    P.emit("tensor", lambda e, h=h, w=w, ec=ec, sl=sl, c0=c0,
                           sw=rw2.cslot(): e.matmul(
                        R.mm_slots[sl][0:64, 0:256],
                        wqr[:, sw, ec, c0:c0 + 64],
                        lnT[:, ec, 256 * h:256 * (h + 1)],
                        start=(ec == 0), stop=(ec == 7)),
                        waits=(wsl if ec == 0 else []),
                        inc=(rp2.fill_inc(1) if ec == 7 else None))
                rp2.commit()
                d = rp2.done()
                f = 192 * w + c0
                col, p0 = f // 128, f % 128
                bidx = 24 * bj + col
                sc = 0.125 if part == 1 else 1.0
                vv = P.emit("vector", lambda e, w=w, sl=sl, dst=dst, p0=p0,
                            bidx=bidx, sc=sc: e.scalar_tensor_tensor(
                    dst[:, 256 * w:256 * (w + 1)], R.mm_slots[sl][0:64, 0:256],
                    sc, bcol[p0:p0 + 64, bidx:bidx + 1].broadcast_to([64, 256]),
                    op0=ALU.mult, op1=ALU.add),
                    waits=rp2.cons(idx=rp2.drains - 1), inc=d)
                dlast = d
            P.emit("vector", lambda e: None, waits=[(dlast[0], vv)],
                   inc=rw2.done())
        rt = Ring(P, "trp", 2, ca=1)
        for g in range(0, NT, 8):
            s, wg = rt.acquire()
            bank = R.psT if s == 0 else R.psT2
            if g == 0:
                wg = wg + [(dlast[0], vv)] + [("rw2_c0", P.val("rw2_c0")),
                                              ("rw2_c1", P.val("rw2_c1"))]
            for k in range(8):
                c = g + k
                P.emit("tensor", lambda e, c=c, k=k, bank=bank: e.transpose(
                    bank[:, 64 * k:64 * (k + 1)],
                    vhb[:, 128 * c:128 * (c + 1)], R.eye_bf[0:64, 0:64]),
                    waits=(wg if k == 0 else []),
                    inc=(rt.fill_inc(1) if k == 7 else None))
            rt.commit()
            wv2 = rt.cons()
            for k in range(8):
                c = g + k
                P.emit("vector", lambda e, c=c, k=k, bank=bank: e.tensor_copy(
                    Vh[:, c, 0, 0:64], bank[:, 64 * k:64 * k + 64]),
                    waits=(wv2 if k == 0 else []),
                    inc=(rt.done() if k == 7 else None))
        P.emit("vector", lambda e: None, waits=rt.all_drained_waits(),
               inc=rqk.fill_inc(1))

    pitch = ctx3.ap[0][0]

    def ctx_slice(h, i0):
        p0 = 64 * (h % 2)
        hc = h // 2
        w0 = i0 // 256
        base = ctx3[p0:p0 + 64, hc, :]
        return bass.AP(base.tensor, base.offset + w0,
                       [[pitch, 64], [1, 2], [16, 256]])

    def qk_load2(h, sq, wq, qh, kh, rqk, rp2):
        qk_load(h, sq, wq, qh, kh, rqk, rp2)

    emit_attn_core(P, R, None, None, Vh, ctx3, nheads=16, dh_qk=64, dh_v=64,
                   causal=False, qk_load=qk_load2, ctx_slice=ctx_slice,
                   vh_map=lambda h: 0, qk_bufs=(qhb, khb))
    P.barrier(name="bar")

    rp = Ring(P, "mm", 4, ca=1)
    wbuf2 = a.alloc([128, 8, 1024], BF16)
    xres = a.alloc([128, 2, D], F32)
    for c in range(8):
        P.emit("sync", lambda e, c=c: e.dma_start(
            wbuf2[:, c, :], woutT[128 * c:128 * (c + 1), :]),
            waits=(wo_dw if c == 0 else []), inc=("lG", 16))
    lwv = P.val("lG")
    rxr = Ring(P, "rxr", 2, ca=16)
    for tt in range(NT):
        sx, wx = rxr.acquire()
        P.emit("sync", lambda e, tt=tt, sx=sx: e.dma_start(
            xres[:, sx, :], x_in[128 * tt:128 * (tt + 1), :]),
            waits=wx, inc=rxr.fill_inc(16))
        rxr.commit()
        stt_w = []
        for hf in range(2):
            sl, w = rp.acquire()
            if tt == 0 and hf == 0:
                w = w + [("lG", lwv)]
            P.emit("tensor", lambda e, hf=hf, sl=sl: e.matmul(
                R.mm_slots[sl][:, :], R.ones_bf[0:1, 0:128],
                bout_row[0:1, 512 * hf:512 * (hf + 1)], start=True, stop=False),
                waits=w, inc=None)
            for dc in range(8):
                P.emit("tensor", lambda e, tt=tt, hf=hf, dc=dc, sl=sl: e.matmul(
                    R.mm_slots[sl][:, :], ctx3[:, dc, 128 * tt:128 * (tt + 1)],
                    wbuf2[:, dc, 512 * hf:512 * (hf + 1)],
                    start=False, stop=(dc == 7)),
                    inc=(rp.fill_inc(1) if dc == 7 else None))
            rp.commit()
            wv = rp.cons() + (rxr.cons() if hf == 0 else [])
            csl = rp.cslot()
            dd = rp.done()
            v = P.emit("vector", lambda e, hf=hf, sl=csl, sx=rxr.cslot():
                       e.scalar_tensor_tensor(
                           xres[:, sx, 512 * hf:512 * (hf + 1)], R.mm_slots[sl][:, :],
                           1.0, xres[:, sx, 512 * hf:512 * (hf + 1)],
                           op0=ALU.mult, op1=ALU.add),
                       waits=wv, inc=dd)
            stt_w.append((dd[0], v))
        P.emit("sync", lambda e, tt=tt, sx=rxr.cslot(): e.dma_start(
            x_out[128 * tt:128 * (tt + 1), :], xres[:, sx, :]),
            waits=stt_w, inc=rxr.done())
    P.emit("gpsimd", lambda e: None, waits=rxr.all_drained_waits())
    a.reset(m0)


def build_L0_ffn(P, R, x_in, x_out, tabs_d, rhalfT_d, sel2_d, sel2T_d, n_inner=6,
                 snap_d=None, snap_qk_d=None, snap_yb_d=None):
    a = R.arena
    m0 = a.mark()
    DI = 2730
    NK = 22
    CW = [min(128, DI - 128 * k) for k in range(NK)]
    yA = P.dram("l0f_y0", [T, D], F32)
    yB = P.dram("l0f_y1", [T, D], F32)
    qsp = P.dram("l0f_qsp", [D, T], BF16)
    ksp = P.dram("l0f_ksp", [D, T], BF16)
    sel2 = P.sbuf("l0f_sel2", [128, 2], F32)
    sel2T = P.sbuf("l0f_sel2T", [2, 128], BF16)
    selq = P.sbuf("l0f_selq", [2, 128], BF16)
    selk = P.sbuf("l0f_selk", [2, 128], BF16)
    g2row = P.sbuf("l0f_g2", [2, 128], BF16)
    rhalf = P.sbuf("l0f_rh", [128, 128], BF16)
    P.emit("gpsimd", lambda e: e.dma_start(sel2[:, :], sel2_d[:, :]), inc=("dG", 16))
    P.emit("gpsimd", lambda e: e.dma_start(sel2T[:, :], sel2T_d[:, :]), inc=("dG", 16))
    P.emit("gpsimd", lambda e: e.dma_start(rhalf[:, :], rhalfT_d[:, :]), inc=("dG", 16))
    P.emit("gpsimd", lambda e: None, waits=P.flow("dG"))

    yst = a.alloc([128, 2, D], F32)
    ry = Ring(P, "ry", 2, ca=16)

    def consume_y0(i, lt, s2, rl):
        sy, wy = ry.acquire()
        vv = P.emit("vector", lambda e, s2=s2, sy=sy: e.tensor_copy(
            yst[:, sy, :], lt[:, s2, :]), waits=rl.cons() + wy, inc=ry.fill_inc(1))
        ry.commit()
        P.emit("vector", lambda e, vv=vv, sy=sy: None,
               waits=[(ry.ps[sy], vv)], inc=rl.done())
        P.emit("sync", lambda e, i=i, sy=ry.cslot(): e.dma_start(
            yA[128 * i:128 * (i + 1), :], yst[:, sy, :]),
            waits=ry.cons(), inc=ry.done())

    emit_ln_tiles(P, R, x_in, 0, T, consume_y0, out_f32=True)
    P.emit("sync", lambda e: None, waits=ry.all_drained_waits())
    P.barrier(name="bar")

    for il in range(n_inner):
        m1 = a.mark()
        shiftT = a.alloc([128, ND, T], BF16)
        Vall = a.alloc([128, 32, 16, 65], BF16)
        m2 = a.mark()
        tabs = a.alloc([128, 2, T], BF16)
        wcol = a.alloc([128, 2, 8, 128], BF16)
        qchunk = a.alloc([128, T], BF16)
        qT_sb = a.alloc([128, 512], BF16)
        sq_sb = a.alloc([128, 512], F32)
        rstd = a.alloc([2, 2, 512], F32)
        rstd_bf = a.alloc([2, 512], BF16)
        tmpf = a.alloc([128, 512], F32)
        tmpg = a.alloc([128, 512], F32)

        def consume_sh(i, lt, s2, rl):
            pairs = [(lt[:, s2, 128 * c:128 * (c + 1)],
                      shiftT[:, c, 128 * i:128 * (i + 1)]) for c in range(ND)]
            emit_transposes(P, R, pairs, extra_waits=rl.cons())
            P.emit("vector", lambda e: None, inc=rl.done())

        emit_ln_tiles(P, R, yA, 0, T, consume_sh, shift=True, ln=False)
        P.emit("sync", lambda e: e.dma_start(tabs.rearrange("p a b -> p (a b)"),
                                             tabs_d[:, 0:2 * T]), inc=("dS", 16))
        for j in range(2):
            for r in range(2):
                P.emit("gpsimd", lambda e, j=j, r=r, il=il: e.dma_start(
                    g2row[j:j + 1, 64 * r:64 * (r + 1)],
                    bass.AP(P.tensors["L0_ffn_q_scale"], 64 * il, [[64, 1], [1, 64]])),
                    inc=("dG", 16))
        P.emit("vector", lambda e: e.tensor_mul(selq[:, :], sel2T[:, :], g2row[:, :]),
               waits=[("dG", P.val("dG"))] + P.flow("lv"), inc=("lv", 1))
        for j in range(2):
            for r in range(2):
                P.emit("gpsimd", lambda e, j=j, r=r, il=il: e.dma_start(
                    g2row[j:j + 1, 64 * r:64 * (r + 1)],
                    bass.AP(P.tensors["L0_ffn_k_scale"], 64 * il, [[64, 1], [1, 64]])),
                    waits=P.flow("lv"), inc=("dG", 16))
        P.emit("vector", lambda e: e.tensor_mul(selk[:, :], sel2T[:, :], g2row[:, :]),
               waits=[("dG", P.val("dG"))], inc=("lv", 1))
        P.emit("vector", lambda e: e.tensor_scalar_mul(selk[:, :], selk[:, :], 8.0),
               waits=P.flow("lv"), inc=("lv", 1))
        P.emit("vector", lambda e: e.memset(Vall[:, :, :, 64:65], 1.0), inc=("lv", 1))
        P.emit("sync", lambda e: None, waits=P.flow("dS"))
        P.barrier(name="bar")

        rp = Ring(P, "mm", 4, ca=1)
        rw = Ring(P, "wc", 2, ca=1)
        for fc in range(24):
            which = fc // 8
            if fc == 8:
                P.emit("sync", lambda e: e.dma_start(
                    tabs.rearrange("p a b -> p (a b)"), tabs_d[:, 2 * T:4 * T]),
                    waits=P.flow("lv2") + P.flow("dQ"), inc=("dS", 16))
            sw, ww = rw.acquire()
            src = bass.AP(P.tensors["L0_ffn_Wqkv"], il * D * 3072 + 128 * fc,
                          [[3072, 128], [3072 * 128, 8], [1, 128]])
            P.emit("gpsimd", lambda e, src=src, sw=sw: e.dma_start(
                wcol[:, sw, :, :], src), waits=ww, inc=rw.fill_inc(16))
            rw.commit()
            if which < 2:
                sel = selq if which == 0 else selk
                sp = qsp if which == 0 else ksp
                last_vv = None
                for tg in range(8):
                    sl, w = rp.acquire()
                    if tg == 0:
                        w = w + rw.cons()
                    for ec in range(8):
                        P.emit("tensor", lambda e, tg=tg, ec=ec, sl=sl,
                               sw=rw.cslot(): e.matmul(
                            R.mm_slots[sl][:, :], wcol[:, sw, ec, :],
                            shiftT[:, ec, 512 * tg:512 * (tg + 1)],
                            start=(ec == 0), stop=(ec == 7)),
                            waits=(w if ec == 0 else []),
                            inc=(rp.fill_inc(1) if ec == 7 else None))
                    rp.commit()
                    d = rp.done()
                    P.emit("scalar", lambda e, sl=sl: e.copy(qT_sb[:, :],
                                                             R.mm_slots[sl][:, :]),
                           waits=rp.cons(idx=rp.drains - 1) + P.flow("lv2"),
                           inc=("ls", 1))
                    P.emit("scalar", lambda e, sl=sl: e.activation(
                        sq_sb[:, :], R.mm_slots[sl][:, :], AF.Square), inc=d)
                    P.emit("tensor", lambda e: e.matmul(
                        R.psM[0:2, 0:512], sel2[:, :], sq_sb[:, :],
                        start=True, stop=True),
                        waits=P.flow("ls") + [(d[0], P.val(d[0]))] + P.flow("lv2"),
                        inc=("lt", 1))
                    P.emit("scalar", lambda e: e.activation(
                        rstd[:, 0, :], R.psM[0:2, 0:512], AF.Sqrt),
                        waits=P.flow("lt"), inc=("ls", 1))
                    P.emit("vector", lambda e: e.tensor_scalar_max(
                        rstd[:, 1, :], rstd[:, 0, :], 1e-12),
                        waits=P.flow("ls"), inc=("lv", 1))

                    def _re(e):
                        with P.nc.allow_low_precision(reason="bcast operand bf16"):
                            return e.reciprocal(rstd_bf[:, :], rstd[:, 1, :])
                    P.emit("vector", _re, waits=P.flow("lv"), inc=("lv", 1))
                    P.emit("tensor", lambda e, sel=sel: e.matmul(
                        R.psM[:, 0:512], sel[:, :], rstd_bf[:, :],
                        start=True, stop=True),
                        waits=P.flow("lv"), inc=("lt", 1))
                    P.emit("vector", lambda e: e.scalar_tensor_tensor(
                        qT_sb[:, :], R.psM[:, 0:512], 1.0, qT_sb[:, :],
                        op0=ALU.mult, op1=ALU.mult),
                        waits=P.flow("lt"), inc=("lv", 1))
                    sl2, w2 = rp.acquire()
                    P.emit("tensor", lambda e, sl2=sl2: e.matmul(
                        R.mm_slots[sl2][:, :], rhalf[:, :], qT_sb[:, :],
                        start=True, stop=True),
                        waits=w2 + P.flow("lv"), inc=rp.fill_inc(1))
                    rp.commit()
                    d2 = rp.done()
                    wtab = [("dS", P.val("dS"))] if tg == 0 else []
                    P.emit("vector", lambda e, sl2=sl2, tg=tg:
                           e.scalar_tensor_tensor(
                               tmpf[:, :], R.mm_slots[sl2][:, :], 1.0,
                               tabs[:, 1, 512 * tg:512 * (tg + 1)],
                               op0=ALU.mult, op1=ALU.mult),
                           waits=rp.cons(idx=rp.drains - 1) + wtab, inc=d2)
                    P.emit("vector", lambda e, tg=tg: e.tensor_tensor(
                        tmpg[:, :], qT_sb[:, :],
                        tabs[:, 0, 512 * tg:512 * (tg + 1)],
                        op=ALU.mult), inc=("lv", 1))
                    last_vv = P.emit("vector", lambda e, tg=tg: e.tensor_add(
                        qchunk[:, 512 * tg:512 * (tg + 1)], tmpf[:, :], tmpg[:, :]),
                        waits=(P.flow("dQ") if tg == 0 else []), inc=("lv2", 1))
                P.emit("sync", lambda e, fc=fc, sp=sp: e.dma_start(
                    sp[128 * (fc % 8):128 * (fc % 8) + 128, :], qchunk[:, :]),
                    waits=[("lv2", last_vv)], inc=("dQ", 16))
                P.emit("vector", lambda e: None, waits=[("lv2", last_vv)],
                       inc=rw.done())
            else:
                hf = fc % 8
                for tt in range(NT):
                    sl, w = rp.acquire()
                    if tt == 0:
                        w = w + rw.cons()
                    for ec in range(8):
                        P.emit("tensor", lambda e, tt=tt, ec=ec, sl=sl,
                               sw=rw.cslot(): e.matmul(
                            R.mm_slots[sl][:, 0:128],
                            shiftT[:, ec, 128 * tt:128 * (tt + 1)],
                            wcol[:, sw, ec, :], start=(ec == 0), stop=(ec == 7)),
                            waits=(w if ec == 0 else []),
                            inc=(rp.fill_inc(1) if ec == 7 else None))
                    rp.commit()
                    d = rp.done()
                    P.emit("vector", lambda e, tt=tt, hf=hf, sl=sl: e.tensor_copy(
                        Vall[:, tt, 2 * hf:2 * hf + 2, 0:64],
                        R.mm_slots[sl][:, 0:128]),
                        waits=rp.cons(idx=rp.drains - 1), inc=d)
                P.emit("vector", lambda e: None,
                       waits=[(f"mm_c{i}", P.val(f"mm_c{i}")) for i in range(4)],
                       inc=rw.done())
            if DBG_ATTN_SERIAL:
                P.emit("sync", lambda e: None,
                       waits=P.flow("dS") + P.flow("dQ"))
                P.emit("gpsimd", lambda e: None,
                       waits=P.flow("dG") + P.flow("lG"))
                P.barrier(name="bar")
        P.barrier(name="bar")

        ctxN = a.at(m1, [128, ND, T], BF16)
        a.reset(m2)
        qh = a.alloc([64, T], BF16)
        kh = a.alloc([64, T], BF16)
        Eb = a.alloc([128, 4, 256], BF16)
        Nm = a.alloc([128, 8], F32)
        Rb = a.alloc([128, 8], F32)
        Vs = a.alloc([128, 4, 65], BF16)
        bc_sb = a.alloc([64, 256], F32)
        linv = a.alloc([1, 256], BF16)
        sc2 = a.alloc([1, 4], BF16)
        rqk = Ring(P, "rqk", 1, ca=1)
        rcx = Ring(P, "rcx", 1, ca=1)
        P.emit("sync", lambda e: None, waits=P.flow("dQ"))
        if snap_qk_d is not None:
            P.emit("sync", lambda e, il=il: e.dma_start(
                snap_qk_d[il * 2 * D:il * 2 * D + D, :], qsp[:, :]),
                inc=("dS", 16))
            P.emit("sync", lambda e, il=il: e.dma_start(
                snap_qk_d[il * 2 * D + D:(il + 1) * 2 * D, :], ksp[:, :]),
                inc=("dS", 16))
        for h in range(16):
            sqs, wq = rqk.acquire()
            P.emit("sync", lambda e, h=h: e.dma_start(
                qh[:, :], qsp[64 * h:64 * (h + 1), :]), waits=wq,
                inc=rqk.fill_inc(16))
            P.emit("sync", lambda e, h=h: e.dma_start(
                kh[:, :], ksp[64 * h:64 * (h + 1), :]), inc=rqk.fill_inc(16))
            rqk.commit()
            wqk = rqk.cons()
            for qb in range(16):
                i0 = 256 * qb
                jc0 = max(0, 2 * qb - 2)
                jcs = list(range(jc0, 2 * qb + 2))
                nj = len(jcs)
                nslots = (nj + 1) // 2
                for ci, jc in enumerate(jcs):
                    sl = ci // 2
                    half = ci % 2
                    w = []
                    if qb == 0 and ci == 0:
                        w = list(wqk)
                    if half == 0:
                        w = w + [(f"sc_c{sl}", P.val(f"sc_c{sl}"))]
                    P.emit("tensor", lambda e, jc=jc, sl=sl, half=half, i0=i0:
                           e.matmul(
                        R.mm_slots[sl][:, 256 * half:256 * (half + 1)],
                        kh[:, 128 * jc:128 * (jc + 1)], qh[:, i0:i0 + 256],
                        start=True, stop=True),
                        waits=w, inc=(f"sc_p{sl}", 1))
                for si in range(nslots):
                    nn = min(2, nj - 2 * si)
                    P.emit("vector", lambda e, si=si, nn=nn: e.reduce_max(
                        Nm[:, si:si + 1], R.mm_slots[si][:, 0:256 * nn],
                        axis=AX.X, negate=True),
                        waits=[(f"sc_p{si}", P.val(f"sc_p{si}"))] + P.flow("ls"),
                        inc=("lv", 1))
                P.emit("vector", lambda e, nslots=nslots: e.tensor_reduce(
                    Nm[:, 4:5], Nm[:, 0:nslots], axis=AX.X, op=ALU.min),
                    waits=P.flow("lv"), inc=("lv", 1))
                for si in range(nslots):
                    nn = min(2, nj - 2 * si)
                    P.emit("scalar", lambda e, si=si, nn=nn: e.activation(
                        Eb[:, 2 * si:2 * si + nn, :],
                        R.mm_slots[si][:, 0:256 * nn], AF.Exp, bias=Nm[:, 4:5]),
                        waits=P.flow("lv"), inc=("ls", 1))
                    P.emit("scalar", lambda e: None, waits=P.flow("ls"),
                           inc=(f"sc_c{si}", 1))
                wm = P.flow("ls")
                if qb == 0:
                    P.emit("gpsimd", lambda e: e.affine_select(
                        Eb[:, 0:2, :], Eb[:, 0:2, :], [[-128, 2], [1, 256]],
                        ALU.is_ge, 0.0, base=0, channel_multiplier=-1),
                        waits=wm, inc=("lg", 1))
                else:
                    P.emit("gpsimd", lambda e: e.affine_select(
                        Eb[:, 0:2, :], Eb[:, 0:2, :], [[128, 2], [-1, 256]],
                        ALU.is_ge, 0.0, base=0, channel_multiplier=1),
                        waits=wm, inc=("lg", 1))
                    P.emit("gpsimd", lambda e: e.affine_select(
                        Eb[:, 2:4, :], Eb[:, 2:4, :], [[-128, 2], [1, 256]],
                        ALU.is_ge, 0.0, base=0, channel_multiplier=-1),
                        inc=("lg", 1))
                P.emit("tensor", lambda e: e.transpose(
                    R.psM[0:1, 0:128], Nm[:, 4:5], R.eye_f[:, :]),
                    waits=P.flow("lv") + P.flow("ls"), inc=("lt", 1))
                P.emit("vector", lambda e: e.tensor_reduce(
                    sc2[0:1, 0:1], R.psM[0:1, 0:128], axis=AX.X, op=ALU.min),
                    waits=P.flow("lt"), inc=("lv", 1))
                P.emit("tensor", lambda e: e.matmul(
                    R.psM[:, 256:257], R.ones_bf[0:1, 0:128], sc2[0:1, 0:1],
                    start=True, stop=True), waits=P.flow("lv"), inc=("lt", 1))
                P.emit("scalar", lambda e: e.copy(Rb[:, 4:5], R.psM[:, 256:257]),
                       waits=P.flow("lt"), inc=("ls", 1))
                P.emit("scalar", lambda e: e.activation(
                    Rb[:, 0:1], Nm[:, 4:5], AF.Exp, scale=-1.0, bias=Rb[:, 4:5]),
                    waits=P.flow("ls"), inc=("ls", 1))
                scx, wcx = rcx.acquire()
                P.emit("vector", lambda e, h=h, jc0=jc0, nj=nj: e.tensor_tensor(
                    Vs[:, 0:nj, :], Vall[:, jc0:jc0 + nj, h, :],
                    Rb[:, 0:1].unsqueeze(1).broadcast_to([128, nj, 65]),
                    op=ALU.mult),
                    waits=P.flow("ls") + wcx, inc=("lv", 1))
                wfirst = P.flow("lv") + P.flow("lg")
                for ci in range(nj):
                    P.emit("tensor", lambda e, ci=ci, nj=nj: e.matmul(
                        R.psC[0:65, 0:256], Vs[:, ci, :], Eb[:, ci, :],
                        start=(ci == 0), stop=(ci == nj - 1)),
                        waits=(wfirst if ci == 0 else []),
                        inc=(("lt", 1) if ci == nj - 1 else None))
                rcx.commit()

                def _re2(e):
                    with P.nc.allow_low_precision(reason="1/l bf16"):
                        return e.reciprocal(linv[0:1, :], R.psC[64:65, 0:256])
                P.emit("vector", _re2, waits=P.flow("lt"), inc=("lv", 1))
                P.emit("tensor", lambda e: e.matmul(
                    R.psM[0:64, 0:256], R.ones_bf[0:1, 0:64], linv[0:1, :],
                    start=True, stop=True), waits=P.flow("lv") + P.flow("ls"),
                    inc=("lt", 1))
                P.emit("scalar", lambda e: e.copy(bc_sb[:, :], R.psM[0:64, 0:256]),
                       waits=P.flow("lt"), inc=("ls", 1))
                dcx = rcx.done()
                vcx = P.emit("vector", lambda e, h=h, i0=i0: e.scalar_tensor_tensor(
                    ctxN[64 * (h % 2):64 * (h % 2) + 64, h // 2, i0:i0 + 256],
                    R.psC[0:64, 0:256], 1.0, bc_sb[:, :],
                    op0=ALU.mult, op1=ALU.mult),
                    waits=P.flow("ls"), inc=dcx)
            P.emit("vector", lambda e: None, waits=[(dcx[0], vcx)], inc=rqk.done())
            if DBG_ATTN_SERIAL:
                P.emit("sync", lambda e: None,
                       waits=P.flow("dS") + P.flow("dQ"))
                P.emit("gpsimd", lambda e: None,
                       waits=P.flow("dG") + P.flow("lG"))
                P.barrier(name="bar")
        P.barrier(name="bar")

        rp = Ring(P, "mm", 4, ca=1)
        wbuf2 = a.alloc([128, 8, 1024], BF16)
        xres = a.alloc([128, 2, D], F32)
        for c in range(8):
            src = bass.AP(P.tensors["L0_ffn_Wo"], il * D * D + 128 * c * D,
                          [[D, 128], [1, D]])
            P.emit("gpsimd", lambda e, c=c, src=src: e.dma_start(
                wbuf2[:, c, :], src), inc=("lG", 16))
        lwv = P.val("lG")
        rxr = Ring(P, "rxr", 2, ca=16)
        for tt in range(NT):
            sx, wx = rxr.acquire()
            P.emit("sync", lambda e, tt=tt, sx=sx: e.dma_start(
                xres[:, sx, :], yA[128 * tt:128 * (tt + 1), :]),
                waits=wx, inc=rxr.fill_inc(16))
            rxr.commit()
            stt_w = []
            for hf in range(2):
                sl, w = rp.acquire()
                if tt == 0 and hf == 0:
                    w = w + [("lG", lwv)]
                for dc in range(8):
                    P.emit("tensor", lambda e, tt=tt, hf=hf, dc=dc, sl=sl: e.matmul(
                        R.mm_slots[sl][:, :], ctxN[:, dc, 128 * tt:128 * (tt + 1)],
                        wbuf2[:, dc, 512 * hf:512 * (hf + 1)],
                        start=(dc == 0), stop=(dc == 7)),
                        waits=(w if dc == 0 else []),
                        inc=(rp.fill_inc(1) if dc == 7 else None))
                rp.commit()
                wv = rp.cons() + (rxr.cons() if hf == 0 else [])
                csl = rp.cslot()
                dd = rp.done()
                v = P.emit("vector", lambda e, hf=hf, sl=csl, sx=rxr.cslot():
                           e.scalar_tensor_tensor(
                               xres[:, sx, 512 * hf:512 * (hf + 1)],
                               R.mm_slots[sl][:, :], 1.0,
                               xres[:, sx, 512 * hf:512 * (hf + 1)],
                               op0=ALU.mult, op1=ALU.add),
                           waits=wv, inc=dd)
                stt_w.append((dd[0], v))
            P.emit("sync", lambda e, tt=tt, sx=rxr.cslot(): e.dma_start(
                yB[128 * tt:128 * (tt + 1), :], xres[:, sx, :]),
                waits=stt_w, inc=rxr.done())
            if DBG_ATTN_SERIAL:
                P.emit("sync", lambda e: None,
                       waits=P.flow("dS") + P.flow("dQ"))
                P.emit("gpsimd", lambda e: None,
                       waits=P.flow("dG") + P.flow("lG"))
                P.barrier(name="bar")
        P.emit("sync", lambda e: None, waits=rxr.all_drained_waits())
        if snap_yb_d is not None:
            P.emit("sync", lambda e, il=il: e.dma_start(
                snap_yb_d[il * T:(il + 1) * T, :], yB[:, :]), inc=("dS", 16))
            P.emit("sync", lambda e: None, waits=P.flow("dS"))
        P.barrier(name="bar")
        a.reset(m1)

        m3 = a.mark()
        gcol = a.alloc([128, 8], F32)
        bco_f = a.alloc([128, 8], F32)
        bco_b = a.alloc([128, 8], BF16)
        b1c = a.alloc([128, 2, NK], F32)
        b2row = a.alloc([1, D], BF16)
        w1col = a.alloc([128, 2, 2, 8, 128], BF16)
        hbuf = a.alloc([128, NK, 1024], BF16)
        w2buf = a.alloc([128, NK, 1024], BF16)
        lnTs = a.alloc([128, ND, 1024], BF16)
        gel = a.alloc([128, 512], F32)
        xres_f = a.alloc([128, 2, D], F32)
        emit_colstage(P, R, bass.AP(P.tensors["L0_ffn_ln_g"], il * D, [[1, D]]),
                      D, gcol)
        emit_colstage(P, R, bass.AP(P.tensors["L0_ffn_ln_b"], il * D, [[1, D]]),
                      D, bco_f)
        P.emit("vector", lambda e: e.tensor_copy(bco_b[:, :], bco_f[:, :]),
               waits=P.flow("lv"), inc=("lv", 1))
        P.emit("gpsimd", lambda e, il=il: e.dma_start(
            b2row[0:1, :],
            bass.AP(P.tensors["L0_ffn_b2"], il * D, [[D, 1], [1, D]])),
            inc=("dG", 16))
        for k in range(NK):
            for j in range(2):
                f0 = 2730 * j + 128 * k
                nwid = CW[k]
                P.emit("sync", lambda e, f0=f0, nwid=nwid, j=j, k=k, il=il:
                       e.dma_start(
                    b1c[0:nwid, j, k:k + 1],
                    bass.AP(P.tensors["L0_ffn_b1"], il * 5460 + f0,
                            [[1, nwid], [1, 1]])),
                    inc=("dS", 16))
        P.emit("sync", lambda e: None, waits=P.flow("dS"))
        P.emit("gpsimd", lambda e: None, waits=P.flow("dG"))
        P.barrier(name="bar")
        rw1 = Ring(P, "wc", 2, ca=1)
        rp = Ring(P, "mm", 4, ca=1)
        rxr = Ring(P, "rxr", 2, ca=16)
        TG = 1024
        for sg in range(4):
            t0 = sg * TG

            def consume_f(i, lt, s2, rl):
                pairs = [(lt[:, s2, 128 * c:128 * (c + 1)],
                          lnTs[:, c, 128 * i:128 * (i + 1)]) for c in range(ND)]
                emit_transposes(P, R, pairs, extra_waits=rl.cons())
                P.emit("vector", lambda e: None, inc=rl.done())

            emit_ln_tiles(P, R, yB, t0, t0 + TG, consume_f, shift=True, ln=True)
            P.barrier(name="bar")

            for k in range(NK):
                nwid = CW[k]
                sw, ww = rw1.acquire()
                for j in range(2):
                    f0 = 2730 * j + 128 * k
                    src = bass.AP(P.tensors["L0_ffn_W1"], il * D * 5460 + f0,
                                  [[5460, 128], [5460 * 128, 8], [1, nwid]])
                    P.emit("gpsimd", lambda e, src=src, sw=sw, j=j, nwid=nwid:
                           e.dma_start(w1col[:, sw, j, :, 0:nwid], src),
                           waits=(list(ww) if j == 0 else []),
                           inc=rw1.fill_inc(16))
                vd = [(rw1.ps[rw1.fills % 2], P.val(rw1.ps[rw1.fills % 2]))]
                if sg == 0:
                    for j in range(2):
                        for ec in range(8):
                            P.emit("tensor", lambda e, j=j, ec=ec, sw=sw,
                                   nwid=nwid: e.matmul(
                                R.psM[0:nwid, 300 + j:301 + j],
                                w1col[:, sw, j, ec, 0:nwid], bco_b[:, ec:ec + 1],
                                start=(ec == 0), stop=(ec == 7)),
                                waits=(vd + P.flow("lv") if ec == 0 else []),
                                inc=(("lt", 1) if ec == 7 else None))
                    vb = P.emit("vector", lambda e, k=k, nwid=nwid: e.tensor_add(
                        b1c[0:nwid, :, k].rearrange("p a -> p a")
                        if False else b1c[0:nwid, 0:2, k],
                        b1c[0:nwid, 0:2, k], R.psM[0:nwid, 300:302]),
                        waits=P.flow("lt"), inc=("lv", 1))
                vv = P.emit("vector", lambda e, sw=sw, nwid=nwid: e.tensor_tensor(
                    w1col[:, sw, :, :, 0:nwid], w1col[:, sw, :, :, 0:nwid],
                    gcol.unsqueeze(1).unsqueeze(3).broadcast_to([128, 2, 8, nwid]),
                    op=ALU.mult), waits=vd + P.flow("lv"), inc=("ppsc", 1))
                rw1.commit()
                wc_ok = [("ppsc", vv)]
                for tg in range(2):
                    sls = []
                    for j in range(2):
                        sl, w = rp.acquire()
                        if tg == 0 and j == 0:
                            w = w + wc_ok
                        for ec in range(8):
                            P.emit("tensor", lambda e, tg=tg, ec=ec, sl=sl, j=j,
                                   nwid=nwid, sw=rw1.cslot(): e.matmul(
                                R.mm_slots[sl][0:nwid, :],
                                w1col[:, sw, j, ec, 0:nwid],
                                lnTs[:, ec, 512 * tg:512 * (tg + 1)],
                                start=(ec == 0), stop=(ec == 7)),
                                waits=(w if ec == 0 else []),
                                inc=(rp.fill_inc(1) if ec == 7 else None))
                        rp.commit()
                        sls.append(sl)
                    d_a = rp.done()
                    d_g = rp.done()
                    P.emit("scalar", lambda e, k=k, sl=sls[1], nwid=nwid:
                           e.activation(
                        gel[0:nwid, :], R.mm_slots[sl][0:nwid, :], AF.Gelu,
                        bias=b1c[0:nwid, 1, k:k + 1]),
                        waits=rp.cons(idx=rp.drains - 1) + P.flow("lv2"),
                        inc=d_g)
                    vv2 = P.emit("vector", lambda e, k=k, tg=tg, sl=sls[0],
                                 nwid=nwid: e.scalar_tensor_tensor(
                        hbuf[0:nwid, k, 512 * tg:512 * (tg + 1)],
                        R.mm_slots[sl][0:nwid, :], b1c[0:nwid, 0, k:k + 1],
                        gel[0:nwid, :], op0=ALU.add, op1=ALU.mult),
                        waits=rp.cons(idx=rp.drains - 2) +
                        [(d_g[0], P.val(d_g[0]))], inc=d_a)
                    P.emit("vector", lambda e: None, waits=[(d_a[0], vv2)],
                           inc=("lv2", 1))
                P.emit("vector", lambda e: None, waits=P.flow("lv2"),
                       inc=rw1.done())
                if DBG_FFN_SERIAL:
                    P.emit("sync", lambda e: None,
                           waits=P.flow("dS") + P.flow("dQ"))
                    P.emit("gpsimd", lambda e: None,
                           waits=P.flow("dG") + P.flow("lG"))
                    P.barrier(name="bar")
            P.barrier(name="bar")
            for c in range(NK):
                src = bass.AP(P.tensors["L0_ffn_W2"], il * DI * D + 128 * c * D,
                              [[D, CW[c]], [1, D]])
                P.emit("gpsimd", lambda e, c=c, src=src: e.dma_start(
                    w2buf[0:CW[c], c, :], src), inc=("lG", 16))
            lwv = P.val("lG")
            for tt in range(TG // 128):
                sx, wx = rxr.acquire()
                P.emit("sync", lambda e, tt=tt, sx=sx, t0=t0: e.dma_start(
                    xres_f[:, sx, :], yB[t0 + 128 * tt:t0 + 128 * (tt + 1), :]),
                    waits=wx, inc=rxr.fill_inc(16))
                rxr.commit()
                stt_w = []
                for hf in range(2):
                    sl, w = rp.acquire()
                    if tt == 0 and hf == 0:
                        w = w + [("lG", lwv)]
                    P.emit("tensor", lambda e, hf=hf, sl=sl: e.matmul(
                        R.mm_slots[sl][:, :], R.ones_bf[0:1, 0:128],
                        b2row[0:1, 512 * hf:512 * (hf + 1)],
                        start=True, stop=False), waits=w)
                    for hc in range(NK):
                        P.emit("tensor", lambda e, tt=tt, hf=hf, hc=hc, sl=sl:
                               e.matmul(
                            R.mm_slots[sl][:, :],
                            hbuf[0:CW[hc], hc, 128 * tt:128 * (tt + 1)],
                            w2buf[0:CW[hc], hc, 512 * hf:512 * (hf + 1)],
                            start=False, stop=(hc == NK - 1)),
                            inc=(rp.fill_inc(1) if hc == NK - 1 else None))
                    rp.commit()
                    wv = rp.cons() + (rxr.cons() if hf == 0 else [])
                    csl = rp.cslot()
                    dd = rp.done()
                    v = P.emit("vector", lambda e, hf=hf, sl=csl, sx=rxr.cslot():
                               e.scalar_tensor_tensor(
                                   xres_f[:, sx, 512 * hf:512 * (hf + 1)],
                                   R.mm_slots[sl][:, :], 1.0,
                                   xres_f[:, sx, 512 * hf:512 * (hf + 1)],
                                   op0=ALU.mult, op1=ALU.add),
                               waits=wv, inc=dd)
                    stt_w.append((dd[0], v))
                P.emit("sync", lambda e, tt=tt, sx=rxr.cslot(), t0=t0: e.dma_start(
                    yA[t0 + 128 * tt:t0 + 128 * (tt + 1), :], xres_f[:, sx, :]),
                    waits=stt_w, inc=rxr.done())
                if DBG_FFN_SERIAL:
                    P.emit("sync", lambda e: None,
                           waits=P.flow("dS") + P.flow("dQ"))
                    P.emit("gpsimd", lambda e: None,
                           waits=P.flow("dG") + P.flow("lG"))
                    P.barrier(name="bar")
            P.barrier(name="bar")
        P.emit("sync", lambda e: None, waits=rxr.all_drained_waits())
        if snap_d is not None:
            P.emit("sync", lambda e, il=il: e.dma_start(
                snap_d[il * T:(il + 1) * T, :], yA[:, :]), inc=("dS", 16))
            P.emit("sync", lambda e: None, waits=P.flow("dS"))
        a.reset(m3)

    m4 = a.mark()
    xr2 = a.alloc([128, 2, 2, D], F32)
    rf = Ring(P, "rxr", 2, ca=16)
    for tt in range(NT):
        sx, wx = rf.acquire()
        P.emit("sync", lambda e, tt=tt, sx=sx: e.dma_start(
            xr2[:, sx, 0, :], x_in[128 * tt:128 * (tt + 1), :]),
            waits=wx, inc=rf.fill_inc(16))
        P.emit("sync", lambda e, tt=tt, sx=sx: e.dma_start(
            xr2[:, sx, 1, :], yA[128 * tt:128 * (tt + 1), :]),
            inc=rf.fill_inc(16))
        rf.commit()
        wv = rf.cons()
        sx2 = rf.cslot()
        dd = rf.done()
        v = P.emit("vector", lambda e, sx=sx2: e.tensor_add(
            xr2[:, sx, 0, :], xr2[:, sx, 0, :], xr2[:, sx, 1, :]),
            waits=wv, inc=("lv", 1))
        P.emit("sync", lambda e, tt=tt, sx=sx2, v=v: e.dma_start(
            x_out[128 * tt:128 * (tt + 1), :], xr2[:, sx, 0, :]),
            waits=[("lv", v)], inc=dd)
    P.emit("gpsimd", lambda e: None, waits=rf.all_drained_waits())
    a.reset(m0)


def make_l0f_consts():
    DH = 64
    W = 256
    half = np.arange(0, DH, 2, dtype=np.float64)
    inv_freq = 1.0 / 10000.0 ** (half / DH)
    t = np.arange(T, dtype=np.float64)
    fr = t[:, None] * inv_freq[None, :]
    freqs = np.concatenate([fr, fr], axis=1)
    base_scale = (half + 0.4 * DH) / (1.4 * DH)
    power = (t - T // 2) / (W // 2)
    scale = base_scale[None, :] ** power[:, None]
    scale = np.concatenate([scale, scale], axis=1)
    cs = np.cos(freqs) * scale
    ss = np.sin(freqs) * scale
    ci = np.cos(freqs) / scale
    si = np.sin(freqs) / scale
    tabs = np.zeros((128, 4, T))
    for p in range(128):
        d = p % 64
        tabs[p, 0] = cs[:, d]
        tabs[p, 1] = ss[:, d]
        tabs[p, 2] = ci[:, d]
        tabs[p, 3] = si[:, d]
    Rh = np.zeros((128, 128))
    for b in range(2):
        for dd in range(32):
            Rh[64 * b + dd, 64 * b + 32 + dd] = -1.0
            Rh[64 * b + 32 + dd, 64 * b + dd] = 1.0
    s2 = np.zeros((128, 2))
    s2[0:64, 0] = 1.0
    s2[64:128, 1] = 1.0
    bfc = lambda x: np.ascontiguousarray(x).astype(BF)
    return {"l0f_tabs": bfc(tabs.reshape(128, 4 * T)),
            "l0f_rhalfT": bfc(Rh.T),
            "l0f_sel2c": s2.astype(np.float32),
            "l0f_sel2Tc": bfc(s2.T)}


def make_l3_sel_consts():
    s2 = np.zeros((128, 2))
    s2[0:64, 0] = 1.0
    s2[64:128, 1] = 1.0
    s4 = np.zeros((128, 4))
    for r in range(4):
        s4[32 * r:32 * (r + 1), r] = 1.0
    bfc = lambda x: np.ascontiguousarray(x).astype(BF)
    return {"l3a_sel2": s2.astype(np.float32), "l3a_sel2T": bfc(s2.T),
            "l3a_sel4": s4.astype(np.float32), "l3a_sel4T": bfc(s4.T)}


def make_l1_rope_consts():
    t = np.arange(T, dtype=np.float64)
    inv = 1.0 / 10000.0 ** (np.arange(0, 32, 2, dtype=np.float64) / 32.0)
    fr = t[:, None] * inv[None, :]
    emb = np.concatenate([fr, fr], axis=1)
    cos = np.cos(emb)
    sin = np.sin(emb)
    A = cos.T.copy()
    B = sin.T.copy()
    tAB = np.concatenate([A[:, None, :], B[:, None, :]], axis=1).reshape(32, 2 * T)
    Rs = np.zeros((32, 32))
    for i in range(16):
        Rs[2 * i, 2 * i + 1] = -1.0
        Rs[2 * i + 1, 2 * i] = 1.0
    bfc = lambda x: np.ascontiguousarray(x).astype(BF)
    return {"l1a_tAB": bfc(tAB), "l1a_rsgn32T": bfc(Rs.T)}


def make_l2_rope_consts():
    t = np.arange(T, dtype=np.float64)
    inv = 1.0 / 10000.0 ** (np.arange(0, 64, 2, dtype=np.float64) / 64.0)
    fr = t[:, None] * inv[None, :]
    cos = np.cos(fr)
    sin = np.sin(fr)
    A = np.zeros((128, T))
    B = np.zeros((128, T))
    for p in range(128):
        d = p % 64
        A[p] = cos[:, d // 2]
        B[p] = sin[:, d // 2] * (1.0 if d % 2 == 1 else -1.0)
    A = np.zeros((128, T))
    B = np.zeros((128, T))
    for p in range(128):
        d = p % 64
        A[p] = cos[:, d // 2]
        B[p] = sin[:, d // 2]
    Rs = np.zeros((128, 128), dtype=np.float64)
    for h in range(2):
        for i in range(32):
            Rs[64 * h + 2 * i, 64 * h + 2 * i + 1] = -1.0
            Rs[64 * h + 2 * i + 1, 64 * h + 2 * i] = 1.0
    bfc = lambda x: x.astype(BF)
    return {"l2a_rA": bfc(A), "l2a_rB": bfc(B),
            "l2a_rAk": bfc(A / 32.0), "l2a_rBk": bfc(B / 32.0),
            "l2a_rsgnT": bfc(Rs.T)}


BLOCK_CONSTS = {
    "L0_ffn": make_l0f_consts,
    "L2_attn": make_l2_rope_consts,
    "L1_attn": make_l1_rope_consts,
    "L3_attn": make_l3_sel_consts,
}

BLOCK_INPUTS = {
    "L0_ffn": [("L0_ffn_Wqkv", [6, D, 3072], F32), ("L0_ffn_q_scale", [6, 64], F32),
               ("L0_ffn_k_scale", [6, 64], F32), ("L0_ffn_Wo", [6, D, D], F32),
               ("L0_ffn_ln_g", [6, D], F32), ("L0_ffn_ln_b", [6, D], F32),
               ("L0_ffn_W1", [6, D, 5460], F32), ("L0_ffn_b1", [6, 5460], F32),
               ("L0_ffn_W2", [6, 2730, D], F32), ("L0_ffn_b2", [6, D], F32)],
    "L0_attn": [("L0_attn_W_qkv", [3072, D], F32), ("L0_attn_b_qkv", [3072], F32),
                ("L0_attn_W_out", [D, D], F32), ("L0_attn_b_out", [D], F32)],
    "L3_attn": [("L3_attn_Wq", [D, D], F32), ("L3_attn_Wkv", [D, 128], F32),
                ("L3_attn_Wk", [128, 512], F32), ("L3_attn_Wv", [128, 512], F32),
                ("L3_attn_Wo", [D, D], F32), ("L3_attn_q_g", [64], F32),
                ("L3_attn_k_g", [32], F32)],
    "L1_attn": [("L1_attn_Wq_down", [D, D], F32), ("L1_attn_Wk_down", [D, D], F32),
                ("L1_attn_Wv_down", [D, D], F32), ("L1_attn_Wq_up_c", [64, 32], F32),
                ("L1_attn_Wq_up_e", [64, 32], F32), ("L1_attn_Wk_up_c", [64, 32], F32),
                ("L1_attn_Wk_up_e", [64, 32], F32), ("L1_attn_Wv_up", [64, 64], F32),
                ("L1_attn_Wc", [D, D], F32)],
    "L2_attn": [("L2_attn_qw", [D, D], F16), ("L2_attn_kw", [D, D], F16),
                ("L2_attn_vw", [D, D], F16), ("L2_attn_ow", [D, D], F16)],
    "L1_ffn": [("L1_ffn_w_gate", [3072, D], I32),
               ("L1_ffn_w_up", [3072, D], I32),
               ("L1_ffn_w_down", [D, 3072], I32)],
    "L2_ffn": [("L2_ffn_w1_w", [4096, D], F32), ("L2_ffn_w1_s", [4096], F32),
               ("L2_ffn_w2_w", [4096, D], F32), ("L2_ffn_w2_s", [4096], F32),
               ("L2_ffn_h1_w", [4096, 4096], F32), ("L2_ffn_h1_s", [4096], F32),
               ("L2_ffn_h2_w", [4096, 4096], F32), ("L2_ffn_h2_s", [4096], F32),
               ("L2_ffn_w3_w", [D, 4096], F32), ("L2_ffn_w3_s", [D], F32)],
    "L3_ffn": [("L3_ffn_w1_w", [4096, D], F32), ("L3_ffn_w1_s", [4096], F32),
               ("L3_ffn_w2_w", [4096, D], F32), ("L3_ffn_w2_s", [4096], F32),
               ("L3_ffn_h1_w", [4096, 4096], F32), ("L3_ffn_h1_s", [4096], F32),
               ("L3_ffn_h2_w", [4096, 4096], F32), ("L3_ffn_h2_s", [4096], F32),
               ("L3_ffn_w3_w", [D, 4096], F32), ("L3_ffn_w3_s", [D], F32)],
}


def build_program(block_list, detect_races=False):
    nc = bass.Bass("TRN2", target_bir_lowering=False,
                   detect_race_conditions=detect_races)
    P = Prog(nc)
    R = Res(P)
    eyed = P.dram("eye_in", [128, 128], F32, kind="ExternalInput")
    R.init_consts(P, eyed)
    P.barrier(name="bar")

    x_in = P.dram("x_in", [T, D], F32, kind="ExternalInput")
    x_out = P.dram("x_out", [T, D], F32, kind="ExternalOutput")
    cur_in = x_in
    scratch = [P.dram("x_tmp0", [T, D], F32), P.dram("x_tmp1", [T, D], F32)]

    for bi, blk in enumerate(block_list):
        out = x_out if bi == len(block_list) - 1 else scratch[bi % 2]
        ins = [P.dram(nm, shp, dt, kind="ExternalInput")
               for nm, shp, dt in BLOCK_INPUTS[blk]]
        cst = []
        if blk in BLOCK_CONSTS:
            carrs = BLOCK_CONSTS[blk]()
            for nm, arr in carrs.items():
                dt = {np.dtype('float32'): F32, np.dtype(BF): BF16}[arr.dtype]
                cst.append(P.dram(nm, list(arr.shape), dt, kind="ExternalInput"))
        if blk == "L1_ffn":
            build_L1_ffn(P, R, cur_in, out, *ins)
        elif blk == "L2_attn":
            build_L2_attn(P, R, cur_in, out, *(ins + cst))
        elif blk == "L1_attn":
            build_L1_attn(P, R, cur_in, out, *(ins + cst))
        elif blk == "L3_attn":
            build_L3_attn(P, R, cur_in, out, *(ins + cst))
        elif blk == "L0_attn":
            build_L0_attn(P, R, cur_in, out, *(ins + cst))
        elif blk == "L0_ffn":
            build_L0_ffn(P, R, cur_in, out, *cst)
        elif blk in ("L2_ffn", "L3_ffn"):
            k = blk[:2]
            wd = {"w1": ins[0], "s1": ins[1], "w2": ins[2], "s2": ins[3],
                  "h1": ins[4], "s1h": ins[5], "h2": ins[6], "s2h": ins[7],
                  "w3": ins[8], "s3": ins[9]}
            build_ternary_ffn(P, R, cur_in, out, wd, k.lower() + "f")
        else:
            raise ValueError(blk)
        if bi != len(block_list) - 1:
            P.barrier(name="bar")
        cur_in = out
    P.barrier(name="bar")
    return P.build()


EYE128 = np.eye(128, dtype=np.float32)

ALL_BLOCKS = ["L0_attn", "L0_ffn", "L1_attn", "L1_ffn", "L2_attn", "L2_ffn",
              "L3_attn", "L3_ffn"]

LAUNCH_GROUPS = [["L0_attn"], ["L0_ffn"], ["L1_attn"], ["L1_ffn"],
                 ["L2_attn"], ["L2_ffn"], ["L3_attn"], ["L3_ffn"]]

_PROG_CACHE = {}
_CONST_CACHE = {}
_CONTIG_CACHE = {}


def _get_prog(key):
    if key not in _PROG_CACHE:
        _PROG_CACHE[key] = build_program(list(key))
    return _PROG_CACHE[key]


def _consts(blk):
    if blk not in _CONST_CACHE:
        _CONST_CACHE[blk] = BLOCK_CONSTS[blk]()
    return _CONST_CACHE[blk]


def _contig(inputs, nm):
    return np.ascontiguousarray(inputs[nm])


def kernel(**inputs):
    x = np.ascontiguousarray(inputs["L0_attn_x"])
    B = x.shape[0]
    cur = [np.ascontiguousarray(x[b]) for b in range(B)]
    for gi, grp in enumerate(LAUNCH_GROUPS):
        nc = _get_prog(tuple(grp))
        in_maps = []
        for b in range(B):
            m = {"eye_in": EYE128, "x_in": cur[b]}
            for blk in grp:
                for nm, shp, dt in BLOCK_INPUTS[blk]:
                    m[nm] = _contig(inputs, nm)
                if blk in BLOCK_CONSTS:
                    m.update(_consts(blk))
            in_maps.append(m)
        res = run_bass_kernel_spmd(nc, in_maps, core_ids=list(range(B)))
        cur = [np.ascontiguousarray(res.results[b]["x_out"]) for b in range(B)]
    return np.stack(cur, axis=0)
